# version 14
# baseline (speedup 1.0000x reference)
import numpy as np
import concourse.bass as bass
import concourse.mybir as mybir
from concourse.bass_utils import run_bass_kernel_spmd

F32 = mybir.dt.float32
BF16 = mybir.dt.bfloat16
AF = mybir.ActivationFunctionType
OP = mybir.AluOpType

P = 128
D = 1024
NCH = D // P
T = 2048
F = 3072
NFC = F // P
TT = 256
NT = T // TT
EPS_LN = 1e-5
EPS_RMS = 1e-6
NPHASE = 5
DETECT_RACES = False
DRAIN_ENGINES = ()



class Plan:
    def __init__(self):
        self.ops = []
        self.count = {}

    def sem(self, name):
        self.count.setdefault(name, 0)
        return name

    def add(self, eng, emit, waits=(), incs=()):
        for (s, v) in waits:
            assert v <= self.count.get(s, 0), (
                f"wait on future token {s}>={v} (planned {self.count.get(s,0)})")
        toks = []
        for (s, a) in incs:
            self.count[s] = self.count.get(s, 0) + a
            toks.append(self.count[s])
        self.ops.append((eng, emit, list(waits), list(incs)))
        return toks[0] if len(toks) == 1 else toks


ENGS = ("tensor", "vector", "scalar", "sync", "gpsimd")




def build_nc():
    nc = bass.Bass("TRN2", target_bir_lowering=False,
                   detect_race_conditions=DETECT_RACES)

    xin = nc.dram_tensor("xin", [NCH, P, T], F32, kind="ExternalInput").ap()
    xout = nc.dram_tensor("xout", [NCH, P, T], F32, kind="ExternalOutput").ap()
    wg = nc.dram_tensor("wg", [NCH, P, D], F32, kind="ExternalInput").ap()
    wv = nc.dram_tensor("wv", [NCH, P, D], F32, kind="ExternalInput").ap()
    wd = nc.dram_tensor("wd", [NCH, P, D], F32, kind="ExternalInput").ap()
    wgate = nc.dram_tensor("wgate", [NFC, P, D], F32, kind="ExternalInput").ap()
    wup = nc.dram_tensor("wup", [NFC, P, D], F32, kind="ExternalInput").ap()
    wout = nc.dram_tensor("wout", [NFC, P, D], F32, kind="ExternalInput").ap()
    wr = nc.dram_tensor("wr", [NCH, P, D], F32, kind="ExternalInput").ap()
    wk = nc.dram_tensor("wk", [NCH, P, D], F32, kind="ExternalInput").ap()
    wvf = nc.dram_tensor("wvf", [NCH, P, D], F32, kind="ExternalInput").ap()
    bmat = nc.dram_tensor("bmat", [NCH, P, D], F32, kind="ExternalInput").ap()
    cmat = nc.dram_tensor("cmat", [NCH, P, D], F32, kind="ExternalInput").ap()
    amat = nc.dram_tensor("amat", [NCH, P, D], F32, kind="ExternalInput").ap()
    vecs = nc.dram_tensor("vecs", [P, 80], F32, kind="ExternalInput").ap()

    import contextlib
    with contextlib.ExitStack() as S:
        acols = max(NCH * T, 2064 + 7 * 8 * TT, 32 * TT,
                    10240 + 2 * D) + 64
        xsb = S.enter_context(nc.sbuf_tensor("xsb", [P, NCH * T], F32))
        arena = S.enter_context(nc.sbuf_tensor("arena", [P, acols], F32))
        prod = S.enter_context(nc.sbuf_tensor("prod", [P, NFC * TT], F32))
        ahi = S.enter_context(nc.sbuf_tensor("ahi", [P, NCH * D], BF16))
        alo = S.enter_context(nc.sbuf_tensor("alo", [P, NCH * D], BF16))
        wbuf0 = S.enter_context(nc.sbuf_tensor("wbuf0", [P, D], F32))
        wbuf1 = S.enter_context(nc.sbuf_tensor("wbuf1", [P, D], F32))
        sqb0 = S.enter_context(nc.sbuf_tensor("sqb0", [P, TT], F32))
        sqb1 = S.enter_context(nc.sbuf_tensor("sqb1", [P, TT], F32))
        vsb = S.enter_context(nc.sbuf_tensor("vsb", [P, 80], F32))
        ones = S.enter_context(nc.sbuf_tensor("ones", [P, 1], F32))
        onesk = S.enter_context(nc.sbuf_tensor("onesk", [1, P], F32))
        srow = S.enter_context(nc.sbuf_tensor("srow", [1, 8 * TT], F32))
        carry = S.enter_context(nc.sbuf_tensor("carry", [P, NCH], F32))
        hhl = S.enter_context(nc.sbuf_tensor("hhl", [P, 4 * NCH], BF16))
        hf = S.enter_context(nc.sbuf_tensor("hf", [P, NCH], F32))
        ztmp = S.enter_context(nc.sbuf_tensor("ztmp", [P, NCH], F32))
        ztmp2 = S.enter_context(nc.sbuf_tensor("ztmp2", [P, NCH], F32))
        evb = S.enter_context(nc.sbuf_tensor("evb", [P, 2 * TT], F32))
        junk = S.enter_context(nc.sbuf_tensor("junk", [P, 8], F32))
        ps = S.enter_context(nc.psum_tensor("ps", [P, 4096], F32))

        sems = {}
        for name in ("ld", "wd0", "wd1", "wd2", "wd3", "pe", "ve", "ac", "smm",
                     "szv", "sth", "sep", "sbf", "hcp", "st"):
            sems[name] = S.enter_context(nc.semaphore(name))

        plan = Plan()
        for n in sems:
            plan.sem(n)

        emit_program(plan, dict(
            xin=xin, xout=xout, wg=wg, wv=wv, wd=wd, wgate=wgate, wup=wup,
            wout=wout, wr=wr, wk=wk, wvf=wvf, bmat=bmat, cmat=cmat, amat=amat,
            vecs=vecs, xsb=xsb, arena=arena, prod=prod, ahi=ahi, alo=alo,
            wbuf=[wbuf0, wbuf1], sqb=[sqb0, sqb1], vsb=vsb, ones=ones,
            onesk=onesk, srow=srow, carry=carry, hhl=hhl, hf=hf, ztmp=ztmp,
            ztmp2=ztmp2, evb=evb, junk=junk, ps=ps))

        block = S.enter_context(nc.Block())

        def mk(eng_name):
            drainy = eng_name in DRAIN_ENGINES
            def body(e):
                first = True
                for (g, emit, waits, incs) in plan.ops:
                    if g != eng_name:
                        continue
                    for (s, v) in waits:
                        e.wait_ge(sems[s], v)
                    if drainy and not first:
                        e.drain()
                    first = False
                    inst = emit(e)
                    for (s, a) in incs:
                        inst.then_inc(sems[s], a)
            return body

        block.tensor(mk("tensor"))
        block.vector(mk("vector"))
        block.scalar(mk("scalar"))
        block.sync(mk("sync"))
        block.gpsimd(mk("gpsimd"))
    return nc


def emit_program(plan, t):
    xsb, arena, prod = t["xsb"], t["arena"], t["prod"]
    ahi, alo = t["ahi"], t["alo"]
    wbuf, sqb = t["wbuf"], t["sqb"]
    vsb, ones, onesk, srow = t["vsb"], t["ones"], t["onesk"], t["srow"]
    carry, hhl, hf = t["carry"], t["hhl"], t["hf"]
    ztmp, ztmp2, evb = t["ztmp"], t["ztmp2"], t["evb"]
    junk = t["junk"]

    def vdummy():
        plan.add("vector", lambda e: e.memset(junk[:, 0:8], 0.0))
    ps = t["ps"]

    NT_ = T // TT

    def bank(j, lo, n):
        return ps[:, j * 512 + lo: j * 512 + lo + n]

    def bankrow(j, lo, n):
        return ps[0:1, j * 512 + lo: j * 512 + lo + n]

    def bank_scan(j, n):
        return ps[:, j * 512: j * 512 + n]

    def xch(c, t0, n):
        return xsb[:, c * T + t0: c * T + t0 + n]

    def ach(c, t0, n):
        return arena[:, c * T + t0: c * T + t0 + n]

    state = {
        "wbuf_free": [[("ld", 0)], [("ld", 0)]],
        "last_sq_pe": 0,
        "stats_free": 0,
    }

    for c in range(NCH):
        plan.add("sync", (lambda e, c=c: e.dma_start(
            out=xch(c, 0, T), in_=t["xin"][c, :, :])), incs=[("ld", 16)])
    plan.add("sync", lambda e: e.dma_start(out=vsb[:, :], in_=t["vecs"][:, :]),
             incs=[("ld", 16)])
    ld_done = plan.count["ld"]
    plan.add("vector", lambda e: e.memset(ones[:, :], 1.0))
    plan.add("vector", lambda e: e.memset(onesk[:, :], 1.0))
    plan.add("vector", lambda e: e.memset(carry[:, :], 0.0))
    plan.add("vector", lambda e: e.memset(t["hhl"][:, :], 0.0),
             waits=[("ld", ld_done)], incs=[("ve", 1)])
    init_tok = plan.count["ve"]


    def stats_pass(src_ap_fn, tile0, n, eps, want_mean, wait_pairs,
                   banks=(0, 1)):
        BS, BQ = banks
        """LN/RMS statistics for token window: computes per-token
        rstd (and -mu*rstd if want_mean) broadcast into psum bank 1
        (cols 0:n rstd-bcast, n:2n negmu-bcast).  src_ap_fn(c) -> [P,n] AP.
        Returns token ("pe", v) of the broadcast matmuls."""
        srcs = [src_ap_fn(c) for c in range(NCH)]
        sq_consumed = [state["last_sq_pe"], state["last_sq_pe"]]
        for c in range(NCH):
            sq = sqb[c % 2]
            w = list(wait_pairs) if c == 0 else []
            if sq_consumed[c % 2]:
                w.append(("pe", sq_consumed[c % 2]))
            plan.add("scalar", (lambda e, src=srcs[c], sq=sq: e.square(
                out=sq[:, 0:n], in_=src)), waits=w, incs=[("ac", 1)])
            ac_tok = plan.count["ac"]
            w2 = [("ac", ac_tok)]
            wfree = ([("ve", state["stats_free"])]
                     if (c == 0 and state["stats_free"]) else [])
            plan.add("tensor", (lambda e, c=c, src=srcs[c], BS=BS: e.matmul(
                bankrow(BS, 0, n), ones[:, :], src,
                start=(c == 0), stop=(c == NCH - 1))),
                waits=(list(wait_pairs) + wfree) if c == 0 else [])
            tok = plan.add("tensor", (lambda e, c=c, sq=sq, BQ=BQ: e.matmul(
                bankrow(BQ, 0, n), ones[:, :], sq[:, 0:n],
                start=(c == 0), stop=(c == NCH - 1))), waits=w2,
                incs=[("pe", 1)])
            sq_consumed[c % 2] = tok
        state["last_sq_pe"] = plan.count["pe"]
        pe_tok = plan.count["pe"]
        mu = srow[0:1, 0:n]
        var = srow[0:1, n:2 * n]
        rstd = srow[0:1, 2 * n:3 * n]
        tmp = srow[0:1, 3 * n:4 * n]
        plan.add("vector", lambda e: e.tensor_scalar(
            out=mu, in0=bankrow(BS, 0, n), scalar1=1.0 / D, scalar2=None,
            op0=OP.mult), waits=[("pe", pe_tok)])
        plan.add("vector", lambda e: e.tensor_scalar(
            out=var, in0=bankrow(BQ, 0, n), scalar1=1.0 / D, scalar2=eps,
            op0=OP.mult, op1=OP.add))
        if want_mean:
            plan.add("vector", lambda e: e.tensor_tensor(
                out=tmp, in0=mu, in1=mu, op=OP.mult))
            plan.add("vector", lambda e: e.drain())
            plan.add("vector", lambda e: e.tensor_tensor(
                out=var, in0=var, in1=tmp, op=OP.subtract))
        plan.add("vector", lambda e: e.drain())
        plan.add("vector", lambda e: e.reciprocal(out=rstd, in_=var),
                 incs=[("ve", 1)])
        ve_tok = plan.count["ve"]
        plan.add("scalar", lambda e: e.sqrt(out=rstd, in_=rstd),
                 waits=[("ve", ve_tok)], incs=[("ac", 1)])
        ac_tok = plan.count["ac"]
        plan.add("vector", lambda e: e.tensor_tensor(
            out=tmp, in0=rstd, in1=rstd, op=OP.mult), waits=[("ac", ac_tok)])
        plan.add("vector", lambda e: e.drain())
        plan.add("vector", lambda e: e.tensor_tensor(
            out=tmp, in0=tmp, in1=var, op=OP.mult))
        plan.add("vector", lambda e: e.drain())
        plan.add("vector", lambda e: e.tensor_scalar(
            out=tmp, in0=tmp, scalar1=-0.5, scalar2=1.5, op0=OP.mult,
            op1=OP.add))
        plan.add("vector", lambda e: e.drain())
        plan.add("vector", lambda e: e.tensor_tensor(
            out=rstd, in0=rstd, in1=tmp, op=OP.mult), incs=[("ve", 1)])
        ve2 = plan.count["ve"]
        if want_mean:
            plan.add("vector", lambda e: e.drain())
            plan.add("vector", lambda e: e.tensor_tensor(
                out=tmp, in0=mu, in1=rstd, op=OP.mult))
            plan.add("vector", lambda e: e.drain())
            plan.add("vector", lambda e: e.tensor_scalar(
                out=tmp, in0=tmp, scalar1=-1.0, scalar2=None, op0=OP.mult),
                incs=[("ve", 1)])
            ve2 = plan.count["ve"]
        plan.add("tensor", lambda e: e.matmul(
            bank(BQ, 0, n), onesk[:, :], rstd, start=True,
            stop=not want_mean), waits=[("ve", ve2)],
            incs=([("pe", 1)] if not want_mean else []))
        if want_mean:
            plan.add("tensor", lambda e: e.matmul(
                bank(BQ, n, n), onesk[:, :], tmp, start=False, stop=True),
                incs=[("pe", 1)])
        return ("pe", plan.count["pe"])

    def stream_w(dram, chunk, which, wait_pairs):
        sem = "wd%d" % which
        eng = "sync" if which == 0 else "gpsimd"
        tok = plan.add(eng, (lambda e, dram=dram, chunk=chunk, which=which:
                             e.dma_start(out=wbuf[which][:, :],
                                         in_=dram[chunk, :, :])),
                       waits=list(wait_pairs) + [(sem, plan.count[sem])],
                       incs=[(sem, 16)])
        return [(sem, tok)]

    def mm_group(out_ap, stat_sb, moving_fn, wait_pairs, extra_last=()):
        movings = [moving_fn(c) for c in range(NCH)]
        for c in range(NCH):
            w = list(wait_pairs) if c == 0 else []
            plan.add("tensor", (lambda e, c=c, m=movings[c]: e.matmul(
                out_ap, stat_sb[:, c * P:(c + 1) * P], m,
                start=(c == 0), stop=(c == NCH - 1))),
                waits=w, incs=([("pe", 1)] if c == NCH - 1 else []))
        return ("pe", plan.count["pe"])

    ph1_wait = [("ld", ld_done), ("ve", init_tok)]

    evb_free = [("ld", 0), ("ld", 0)]
    ln_done_tok = {}

    for ti in range(NT_):
        t0 = ti * TT
        w0 = ph1_wait if ti == 0 else []

        btok = stats_pass(lambda c: xch(c, t0, TT), t0, TT, EPS_LN, True, w0)

        for c in range(NCH):
            plan.add("vector", (lambda e, o=ach(c, t0, TT),
                                i0=xch(c, t0, TT): e.tensor_tensor(
                out=o, in0=i0, in1=bank(1, 0, TT),
                op=OP.mult)), waits=[btok] if c == 0 else [])
        for c in range(NCH):
            plan.add("vector", (lambda e, o=ach(c, t0, TT): e.tensor_tensor(
                out=o, in0=o, in1=bank(1, TT, TT),
                op=OP.add)))
        plan.add("vector", lambda e: e.drain(), incs=[("ve", 1)])
        ln_tok = ("ve", plan.count["ve"])
        state["stats_free"] = plan.count["ve"]

        btok = stats_pass(lambda c: ach(c, t0, TT), t0, TT, EPS_RMS, False,
                          [ln_tok])
        for c in range(NCH):
            plan.add("vector", (lambda e, c=c, i0=ach(c, t0, TT):
                                e.tensor_tensor(
                out=prod[:, c * TT:(c + 1) * TT], in0=i0,
                in1=bank(1, 0, TT), op=OP.mult)),
                waits=[btok] if c == 0 else [])
        for c in range(NCH):
            plan.add("vector", (lambda e, c=c: e.tensor_scalar(
                out=prod[:, c * TT:(c + 1) * TT],
                in0=prod[:, c * TT:(c + 1) * TT],
                scalar1=vsb[:, 24 + c:25 + c], scalar2=None, op0=OP.mult)))
        plan.add("vector", lambda e: e.drain(), incs=[("ve", 1)])
        xn_tok = ("ve", plan.count["ve"])
        state["stats_free"] = plan.count["ve"]

        if ti == 0:
            gvd_prev = [0, 0, 0]
            xs_prev = 0
        for j in range(NCH):
            for m, dram in enumerate((t["wg"], t["wv"], t["wd"])):
                which = (j * 3 + m) % 2
                wtok = stream_w(dram, j, which, state["wbuf_free"][which])
                bk = 2 + m
                mmtok = mm_group(
                    bank(bk, 0, TT), wbuf[which],
                    (lambda c, j=j: prod[:, c * TT:(c + 1) * TT]),
                    wtok + [xn_tok] + ([("ac", gvd_prev[m])]
                                      if gvd_prev[m] else []))
                state["wbuf_free"][which] = [mmtok]
                ev_guard = [("ve", xs_prev)] if (m < 2 and xs_prev) else []
                if m == 0:
                    plan.add("scalar", (lambda e, j=j: e.activation(
                        out=evb[:, 0:TT], in_=bank(2, 0, TT), func=AF.Sigmoid,
                        bias=vsb[:, 0 + j:1 + j], scale=1.0)),
                        waits=[mmtok] + ev_guard, incs=[("ac", 1)])
                elif m == 1:
                    plan.add("scalar", (lambda e, j=j: e.activation(
                        out=evb[:, TT:2 * TT], in_=bank(3, 0, TT),
                        func=AF.Tanh, bias=vsb[:, 8 + j:9 + j], scale=1.0)),
                        waits=[mmtok] + ev_guard, incs=[("ac", 1)])
                else:
                    plan.add("scalar", (lambda e, j=j: e.activation(
                        out=prod[:, (16 + j) * TT:(17 + j) * TT],
                        in_=bank(4, 0, TT), func=AF.Sigmoid,
                        bias=vsb[:, 16 + j:17 + j], scale=1.0)),
                        waits=[mmtok], incs=[("ac", 1)])
                gvd_prev[m] = plan.count["ac"]
            ac_j = plan.count["ac"]
            xs_prev = plan.add("vector", (lambda e, j=j: e.tensor_tensor(
                out=prod[:, (8 + j) * TT:(9 + j) * TT], in0=evb[:, 0:TT],
                in1=evb[:, TT:2 * TT], op=OP.mult)),
                waits=[("ac", ac_j)], incs=[("ve", 1)])
            plan.add("vector", (lambda e, j=j: e.tensor_scalar(
                out=prod[:, (16 + j) * TT:(17 + j) * TT],
                in0=prod[:, (16 + j) * TT:(17 + j) * TT],
                scalar1=0.998, scalar2=0.001, op0=OP.mult, op1=OP.add)),
                incs=[("ve", 1)])
        for c in range(NCH):
            plan.add("vector", (lambda e, c=c: e.tensor_tensor_scan(
                out=evb[:, 0:TT], data0=prod[:, (16 + c) * TT:(17 + c) * TT],
                data1=prod[:, (8 + c) * TT:(9 + c) * TT],
                initial=carry[:, c:c + 1], op0=OP.mult, op1=OP.add)))
            plan.add("vector", lambda e: e.drain())
            plan.add("vector", (lambda e, c=c: e.tensor_copy(
                out=carry[:, c:c + 1], in_=evb[:, TT - 1:TT])))
            plan.add("vector", (lambda e, o=ach(c, t0, TT): e.tensor_tensor(
                out=o, in0=o, in1=evb[:, 0:TT],
                op=OP.add)))
            plan.add("vector", lambda e: e.drain())
            plan.add("vector", (lambda e, o=xch(c, t0, TT),
                                i1=ach(c, t0, TT): e.tensor_tensor(
                out=o, in0=o, in1=i1,
                op=OP.add)))
        plan.add("vector", lambda e: e.drain(), incs=[("ve", 1)])
        ph1_wait = [("ve", plan.count["ve"])]

    mix_done = [("ve", plan.count["ve"])]
    if NPHASE == 1:
        return finish(plan, t, xch, plan.count["ve"])

    for ti in range(NT_):
        t0 = ti * TT
        w0 = mix_done if ti == 0 else []
        btok = stats_pass(lambda c: ach(c, t0, TT), t0, TT, EPS_RMS, False, w0)
        for c in range(NCH):
            plan.add("vector", (lambda e, o=ach(c, t0, TT): e.tensor_tensor(
                out=o, in0=o, in1=bank(1, 0, TT),
                op=OP.mult)), waits=[btok] if c == 0 else [])
        for c in range(NCH):
            plan.add("vector", (lambda e, c=c, o=ach(c, t0, TT):
                                e.tensor_scalar(
                out=o, in0=o,
                scalar1=vsb[:, 32 + c:33 + c], scalar2=None, op0=OP.mult)))
        plan.add("vector", lambda e: e.drain(), incs=[("ve", 1)])
        xn2_tok = ("ve", plan.count["ve"])
        state["stats_free"] = plan.count["ve"]

        if ti == 0:
            silu_prev = None
            prod_prev = None
            evac_prev = None
        for fj in range(NFC):
            wt_g = stream_w(t["wgate"], fj, 0, state["wbuf_free"][0])
            tok_g = mm_group(bank(2, 0, TT), wbuf[0],
                             (lambda c: ach(c, t0, TT)),
                             wt_g + [xn2_tok] +
                             ([silu_prev] if silu_prev else []))
            state["wbuf_free"][0] = [tok_g]
            plan.add("scalar", (lambda e: e.activation(
                out=evb[:, 0:TT], in_=bank(2, 0, TT), func=AF.Silu)),
                waits=[tok_g] + ([prod_prev] if prod_prev else []),
                incs=[("ac", 1)])
            silu_prev = ("ac", plan.count["ac"])
            wt_u = stream_w(t["wup"], fj, 1, state["wbuf_free"][1])
            tok_u = mm_group(bank(3, 0, TT), wbuf[1],
                             (lambda c: ach(c, t0, TT)),
                             wt_u + ([prod_prev] if prod_prev else []))
            state["wbuf_free"][1] = [tok_u]
            plan.add("vector", (lambda e, fj=fj: e.tensor_tensor(
                out=prod[:, fj * TT:(fj + 1) * TT], in0=evb[:, 0:TT],
                in1=bank(3, 0, TT), op=OP.mult)),
                waits=[tok_u, silu_prev], incs=[("ve", 1)])
            prod_prev = ("ve", plan.count["ve"])

        for half in range(2):
            for fj in range(NFC):
                which = fj % 2
                sem_o = "wd%d" % which
                wt_o = plan.add("sync" if which == 0 else "gpsimd",
                                (lambda e, fj=fj, half=half,
                                         which=which: e.dma_start(
                    out=wbuf[which][:, 0:512],
                    in_=t["wout"][fj, :, half * 512:(half + 1) * 512])),
                    waits=state["wbuf_free"][which] +
                    [(sem_o, plan.count[sem_o])],
                    incs=[(sem_o, 16)])
                for dc in range(4):
                    w = []
                    if dc == 0:
                        w.append(("wd%d" % which, wt_o))
                    if fj == 0 and dc == 0:
                        w.append(prod_prev)
                        if evac_prev is not None:
                            w.append(evac_prev)
                    tok = plan.add("tensor", (lambda e, fj=fj, dc=dc,
                                              which=which: e.matmul(
                        bank(4 + dc, 0, TT),
                        wbuf[which][:, dc * P:(dc + 1) * P],
                        prod[:, fj * TT:(fj + 1) * TT],
                        start=(fj == 0), stop=(fj == NFC - 1))),
                        waits=w,
                        incs=([("pe", 1)] if dc == 3 else []))
                state["wbuf_free"][which] = [("pe", plan.count["pe"])]
            accum_tok = ("pe", plan.count["pe"])
            for dc in range(4):
                c = half * 4 + dc
                plan.add("vector", (lambda e, dc=dc, o=xch(c, t0, TT):
                                    e.tensor_tensor(
                    out=o, in0=o,
                    in1=bank(4 + dc, 0, TT), op=OP.add)),
                    waits=[accum_tok] if dc == 0 else [])
            plan.add("vector", lambda e: e.drain(), incs=[("ve", 1)])
            evac_prev = ("ve", plan.count["ve"])

    ffn1_done = [("ve", plan.count["ve"])]
    if NPHASE == 2:
        return finish(plan, t, xch, plan.count["ve"])

    def emit_ffn(begin_wait):
        LNB = 0
        XR0 = 2064
        XK0 = XR0 + 8 * TT
        XR1 = XK0 + 8 * TT
        XK1 = XR1 + 8 * TT
        TMP = XK1 + 8 * TT
        RB = TMP + 8 * TT
        KA = RB + 8 * TT

        def lnb(c, o, n):
            return arena[:, LNB + c * (TT + 1) + o: LNB + c * (TT + 1) + o + n]

        def ar(base, c, n=TT):
            return arena[:, base + c * TT: base + c * TT + n]

        def prep(ti, w0):
            par = ti % 2
            bset = (0, 1) if par == 0 else (5, 6)
            BQ = bset[1]
            XRp = XR0 if par == 0 else XR1
            XKp = XK0 if par == 0 else XK1
            t0 = ti * TT
            for c in range(NCH):
                if ti == 0:
                    plan.add("vector", (lambda e, c=c: e.memset(
                        lnb(c, 0, 1), 0.0)), waits=w0 if c == 0 else [])
                else:
                    plan.add("vector", (lambda e, c=c: e.tensor_copy(
                        out=lnb(c, 0, 1), in_=lnb(c, TT, 1))),
                        waits=w0 if c == 0 else [])
            plan.add("vector", lambda e: e.drain(), incs=[("ve", 1)])
            sv_tok = ("ve", plan.count["ve"])
            btok = stats_pass(lambda c, t0=t0: xch(c, t0, TT), t0, TT,
                              EPS_LN, True, [sv_tok], banks=bset)
            for c in range(NCH):
                plan.add("vector", (lambda e, c=c, i0=xch(c, t0, TT), BQ=BQ:
                                    e.tensor_tensor(
                    out=lnb(c, 1, TT), in0=i0,
                    in1=bank(BQ, 0, TT), op=OP.mult)),
                    waits=[btok] if c == 0 else [])
            for c in range(NCH):
                plan.add("vector", (lambda e, c=c, BQ=BQ: e.tensor_tensor(
                    out=lnb(c, 1, TT), in0=lnb(c, 1, TT),
                    in1=bank(BQ, TT, TT),
                    op=OP.add)))
            for c in range(NCH):
                plan.add("vector", (lambda e, c=c: e.tensor_tensor(
                    out=ar(TMP, c), in0=lnb(c, 1, TT), in1=lnb(c, 0, TT),
                    op=OP.subtract)))
            for c in range(NCH):
                plan.add("vector", (lambda e, c=c, XRp=XRp:
                                    e.scalar_tensor_tensor(
                    out=ar(XRp, c), in0=ar(TMP, c),
                    scalar=vsb[:, 40 + c:41 + c],
                    in1=lnb(c, 0, TT), op0=OP.mult, op1=OP.add)))
            for c in range(NCH):
                plan.add("vector", (lambda e, c=c, XKp=XKp:
                                    e.scalar_tensor_tensor(
                    out=ar(XKp, c), in0=ar(TMP, c),
                    scalar=vsb[:, 48 + c:49 + c],
                    in1=lnb(c, 0, TT), op0=OP.mult, op1=OP.add)))
            plan.add("vector", lambda e: e.drain(), incs=[("ve", 1)])
            state["stats_free"] = plan.count["ve"]
            return ("ve", plan.count["ve"]), XRp, XKp

        nxt = prep(0, list(begin_wait))
        for ti in range(NT_):
            t0 = ti * TT
            xrk_tok, XRp, XKp = nxt
            if ti == 0:
                r_prev = None
                k_prev = None
                o_prev = None
            for j in range(NCH):
                wt_r = stream_w(t["wr"], j, 0, state["wbuf_free"][0])
                tok_r = mm_group(bank(2, 0, TT), wbuf[0],
                                 (lambda c, XRp=XRp: ar(XRp, c)),
                                 wt_r + [xrk_tok] +
                                 ([r_prev] if r_prev else []))
                state["wbuf_free"][0] = [tok_r]
                plan.add("scalar", (lambda e, j=j: e.activation(
                    out=ar(RB, j), in_=bank(2, 0, TT), func=AF.Sigmoid)),
                    waits=[tok_r], incs=[("ac", 1)])
                r_prev = ("ac", plan.count["ac"])
                wt_k = stream_w(t["wk"], j, 1, state["wbuf_free"][1])
                tok_k = mm_group(bank(3, 0, TT), wbuf[1],
                                 (lambda c, XKp=XKp: ar(XKp, c)),
                                 wt_k + ([k_prev] if k_prev else []))
                state["wbuf_free"][1] = [tok_k]
                plan.add("scalar", (lambda e, j=j: e.activation(
                    out=evb[:, 0:TT], in_=bank(3, 0, TT), func=AF.Relu)),
                    waits=[tok_k], incs=[("ac", 1)])
                plan.add("scalar", (lambda e, j=j: e.square(
                    out=ar(KA, j), in_=evb[:, 0:TT])), incs=[("ac", 1)])
                k_prev = ("ac", plan.count["ac"])
            ka_tok = k_prev

            if ti + 1 < NT_:
                nxt = prep(ti + 1, [])

            for j in range(NCH):
                wt_v = stream_w(t["wvf"], j, j % 2, state["wbuf_free"][j % 2])
                tok_v = mm_group(bank(2 + (j % 2), 0, TT), wbuf[j % 2],
                                 (lambda c: ar(KA, c)), wt_v + [ka_tok] +
                                 ([] if (ti == 0 and j == 0) else [o_prev]))
                state["wbuf_free"][j % 2] = [tok_v]
                plan.add("vector", (lambda e, j=j: e.tensor_tensor(
                    out=ar(TMP, j), in0=ar(RB, j),
                    in1=bank(2 + (j % 2), 0, TT), op=OP.mult)),
                    waits=[tok_v], incs=[("ve", 1)])
                plan.add("vector", lambda e: e.drain())
                plan.add("vector", (lambda e, j=j, o=xch(j, t0, TT):
                                    e.tensor_tensor(
                    out=o, in0=o,
                    in1=ar(TMP, j), op=OP.add)), incs=[("ve", 1)])
                o_prev = ("ve", plan.count["ve"])
        return [("ve", plan.count["ve"])]

    ffn2_done = emit_ffn(ffn1_done)
    if NPHASE == 3:
        return finish(plan, t, xch, plan.count["ve"])

    plan.add("sync", lambda e: e.nop(),
             waits=[("wd%d" % k, plan.count["wd%d" % k]) for k in range(4)])
    for g in range(2):
        for c in range(4):
            ch = g * 4 + c
            tok = plan.add("sync", (lambda e, ch=ch, c=c: e.dma_start(
                out=prod[:, c * D:(c + 1) * D], in_=t["amat"][ch, :, :])),
                waits=(list(ffn2_done) if (g == 0 and c == 0) else
                       [sp_tok] if c == 0 else []) +
                [("wd0", plan.count["wd0"])], incs=[("wd0", 16)])
        wd_tok = plan.count["wd0"]
        for c in range(4):
            ch = g * 4 + c
            plan.add("vector", (lambda e, ch=ch, c=c: e.tensor_copy(
                out=ahi[:, ch * D:(ch + 1) * D],
                in_=prod[:, c * D:(c + 1) * D])),
                waits=[("wd0", wd_tok)] if c == 0 else [])
        for c in range(4):
            ch = g * 4 + c
            plan.add("vector", (lambda e, ch=ch, c=c: e.tensor_tensor(
                out=alo[:, ch * D:(ch + 1) * D],
                in0=prod[:, c * D:(c + 1) * D],
                in1=ahi[:, ch * D:(ch + 1) * D], op=OP.subtract)))
        plan.add("vector", lambda e: e.drain(), incs=[("ve", 1)])
        sp_tok = ("ve", plan.count["ve"])
    a_ready = sp_tok

    for j in range(6):
        plan.add("sync", (lambda e, j=j: e.dma_start(
            out=prod[:, j * D:(j + 1) * D], in_=t["cmat"][j, :, :])),
            waits=([a_ready, ("wd0", plan.count["wd0"])] if j == 0 else []),
            incs=[("wd0", 16)])
    for j in range(6, NCH):
        plan.add("sync", (lambda e, j=j: e.dma_start(
            out=arena[:, 10240 + (j - 6) * D:10240 + (j - 5) * D],
            in_=t["cmat"][j, :, :])), incs=[("wd0", 16)])
    bc_tok = ("wd0", plan.count["wd0"])

    def cstat(j):
        if j < 6:
            return prod[:, j * D:(j + 1) * D]
        return arena[:, 10240 + (j - 6) * D:10240 + (j - 5) * D]

    XLN = 0
    XB = 8 * TT
    HB = 16 * TT
    PRE = 24 * TT

    def arn(base, c, n=TT):
        return arena[:, base + c * TT: base + c * TT + n]

    gs = 0
    ph3_wait = [a_ready]
    ys_prev = None
    for ti in range(NT_):
        t0 = ti * TT
        w0 = ph3_wait if ti == 0 else []
        btok = stats_pass(lambda c: xch(c, t0, TT), t0, TT, EPS_LN, True, w0)
        for c in range(NCH):
            plan.add("vector", (lambda e, c=c, i0=xch(c, t0, TT):
                                e.tensor_tensor(
                out=arn(XLN, c), in0=i0, in1=bank(1, 0, TT),
                op=OP.mult)), waits=[btok] if c == 0 else [])
        for c in range(NCH):
            plan.add("vector", (lambda e, c=c: e.tensor_tensor(
                out=arn(XLN, c), in0=arn(XLN, c), in1=bank(1, TT, TT),
                op=OP.add)))
        plan.add("vector", lambda e: e.drain(), incs=[("ve", 1)])
        xln_tok = ("ve", plan.count["ve"])
        state["stats_free"] = plan.count["ve"]

        for j in range(NCH):
            wt_b = stream_w(t["bmat"], j, j % 2,
                            state["wbuf_free"][j % 2])
            tok_b = mm_group(bank(2 + (j % 2), 0, TT), wbuf[j % 2],
                             (lambda c: arn(XLN, c)),
                             wt_b + [xln_tok] +
                             ([bc_tok] if (ti == 0 and j == 0) else []) +
                             ([] if (ti == 0 and j <= 1) else [xb_prev]))
            state["wbuf_free"][j % 2] = [tok_b]
            plan.add("vector", (lambda e, j=j: e.tensor_copy(
                out=arn(XB, j), in_=bank(2 + (j % 2), 0, TT))),
                waits=[tok_b], incs=[("ve", 1)])
            xb_prev = ("ve", plan.count["ve"])
        xb_tok = xb_prev

        for s in range(TT):
            br = (gs % 2) * 2 * NCH
            bw = ((gs + 1) % 2) * 2 * NCH

            def scan_mm(j, i, w, start, stop, smm_inc):
                til = j * NCH + i
                plan.add("tensor", (lambda e, til=til, br=br, i=i,
                                    start=start: e.matmul(
                    bank_scan(til // NCH, 2),
                    ahi[:, til * P:(til + 1) * P],
                    hhl[:, br + i * 2:br + i * 2 + 2],
                    start=start, stop=False)),
                    waits=w)
                plan.add("tensor", (lambda e, til=til, br=br, i=i,
                                    stop=stop: e.matmul(
                    bank_scan(til // NCH, 1),
                    alo[:, til * P:(til + 1) * P],
                    hhl[:, br + i * 2:br + i * 2 + 1], start=False,
                    stop=stop)),
                    incs=([("smm", 1)] if smm_inc else []))

            for j in range(4, NCH):
                for i in range(4, NCH):
                    w = []
                    if j == 4 and i == 4:
                        if gs == 0:
                            w = [xb_tok, a_ready]
                        else:
                            w = [("sep", 2 * gs - 1)]
                            if s == 0:
                                w.append(xb_tok)
                    scan_mm(j, i, w, start=(i == 4), stop=False, smm_inc=False)
            for j in range(4, NCH):
                for i in range(4):
                    w = [("sep", 2 * gs)] if (j == 4 and i == 0 and gs > 0) \
                        else []
                    scan_mm(j, i, w, start=False, stop=(i == 3),
                            smm_inc=(j == NCH - 1 and i == 3))
            for j in range(4):
                for i in range(4, NCH):
                    w = [("sbf", gs)] if (j == 0 and i == 4 and gs > 0) else []
                    scan_mm(j, i, w, start=(i == 4), stop=False, smm_inc=False)
            for j in range(4):
                for i in range(4):
                    scan_mm(j, i, [], start=False, stop=(i == 3),
                            smm_inc=(j == 3 and i == 3))

            hap = [arena[:, HB + 4 * h * TT + s: HB + (4 * h + 3) * TT
                          + s + 1:TT] for h in range(2)]
            for h in (1, 0):
                plan.add("vector", (lambda e, h=h: e.tensor_copy(
                    out=ztmp[:, 4 * h:4 * h + 4],
                    in_=ps[:, 2048 * h + 1:2048 * h + 1538:512])),
                    waits=[("smm", 2 * gs + (1 if h == 1 else 2))])
                vdummy()
                plan.add("vector", (lambda e, h=h: e.tensor_tensor(
                    out=ztmp2[:, 4 * h:4 * h + 4],
                    in0=ps[:, 2048 * h:2048 * h + 1537:512],
                    in1=ztmp[:, 4 * h:4 * h + 4],
                    op=OP.add)),
                    incs=([("sbf", 1)] if h == 0 else []))
                vdummy()
                plan.add("vector", (lambda e, h=h, s=s: e.tensor_tensor(
                    out=ztmp[:, 4 * h:4 * h + 4],
                    in0=ztmp2[:, 4 * h:4 * h + 4],
                    in1=arena[:, XB + 4 * h * TT + s: XB + (4 * h + 3) * TT
                              + s + 1:TT],
                    op=OP.add)), incs=[("szv", 1)])
                szv_v = plan.count["szv"]
                plan.add("scalar", (lambda e, h=h: e.activation(
                    out=ztmp2[:, 4 * h:4 * h + 4],
                    in_=ztmp[:, 4 * h:4 * h + 4],
                    func=AF.Tanh)), waits=[("szv", szv_v)])
                sth_v = plan.add("scalar", (lambda e, h=h, bw=bw: e.copy(
                    out=hhl[:, bw + 8 * h:bw + 8 * h + 7:2],
                    in_=ztmp2[:, 4 * h:4 * h + 4])),
                    incs=[("sth", 1)])
                plan.add("vector", (lambda e, h=h, bw=bw: e.tensor_tensor(
                    out=hhl[:, bw + 8 * h + 1:bw + 8 * h + 8:2],
                    in0=ztmp2[:, 4 * h:4 * h + 4],
                    in1=hhl[:, bw + 8 * h:bw + 8 * h + 7:2],
                    op=OP.subtract)),
                    waits=[("sth", sth_v)], incs=[("sep", 1)])
                plan.add("vector", (lambda e, h=h, ho=hap[h]: e.tensor_copy(
                    out=ho, in_=ztmp2[:, 4 * h:4 * h + 4])),
                    incs=[("hcp", 1)])
            gs += 1
        scan_tok = ("sep", plan.count["sep"])
        hcp_tok = ("hcp", plan.count["hcp"])

        for j in range(NCH):
            tok_c = mm_group(bank(2 + (j % 2), 0, TT), cstat(j),
                             (lambda c: arn(HB, c)), [scan_tok, hcp_tok] +
                             ([] if ys_prev is None else [ys_prev]))
            plan.add("vector", (lambda e, j=j: e.tensor_tensor(
                out=arn(PRE, j), in0=arn(XLN, j),
                in1=bank(2 + (j % 2), 0, TT), op=OP.add)),
                waits=[tok_c], incs=[("ve", 1)])
            ys_prev = ("ve", plan.count["ve"])

        btok = stats_pass(lambda c: arn(PRE, c), t0, TT, EPS_LN, True,
                          [ys_prev])
        T3T = 32 * TT
        for c in range(NCH):
            plan.add("vector", (lambda e, c=c: e.tensor_tensor(
                out=arn(T3T, c), in0=arn(PRE, c), in1=bank(1, 0, TT),
                op=OP.mult)), waits=[btok] if c == 0 else [])
        for c in range(NCH):
            plan.add("vector", (lambda e, c=c: e.tensor_tensor(
                out=arn(T3T, c), in0=arn(T3T, c), in1=bank(1, TT, TT),
                op=OP.add)))
        for c in range(NCH):
            plan.add("vector", (lambda e, c=c: e.tensor_scalar(
                out=arn(T3T, c), in0=arn(T3T, c),
                scalar1=vsb[:, 56 + c:57 + c], scalar2=vsb[:, 64 + c:65 + c],
                op0=OP.mult, op1=OP.add)))
        for c in range(NCH):
            plan.add("vector", (lambda e, c=c, o=xch(c, t0, TT):
                                e.tensor_tensor(
                out=o, in0=o, in1=arn(T3T, c),
                op=OP.add)))
        plan.add("vector", lambda e: e.drain(), incs=[("ve", 1)])
        ph3_wait = [("ve", plan.count["ve"])]
        state["stats_free"] = plan.count["ve"]

    attnb_done = [("ve", plan.count["ve"])]
    if NPHASE == 4:
        return finish(plan, t, xch, plan.count["ve"])

    emit_ffn(attnb_done)
    final_tok = plan.count["ve"]

    finish(plan, t, xch, final_tok)


def finish(plan, t, xch, final_tok):
    for c in range(NCH):
        plan.add("sync", (lambda e, c=c: e.dma_start(
            out=t["xout"][c, :, :], in_=xch(c, 0, T))),
            waits=[("ve", final_tok)] if c == 0 else [], incs=[("st", 16)])
    st_done = plan.count["st"]
    plan.add("sync", lambda e: e.nop(), waits=[("st", st_done)])




def _tileize(wt, n_out_chunks):
    K, M = wt.shape
    nk = K // P
    arr = wt.reshape(nk, P, n_out_chunks, P).transpose(2, 1, 0, 3)
    return np.ascontiguousarray(arr.reshape(n_out_chunks, P, K))


def _vecpack(d):
    v = np.zeros((P, 80), np.float32)
    names = ["bg", "bv", "bd", "rmw", "rfw", "mur", "muk", "gam", "bet"]
    for i, n in enumerate(names):
        v[:, i * 8:(i + 1) * 8] = d[n].reshape(NCH, P).T
    return v


def kernel(**inp):
    x = np.asarray(inp["L0_attn_a_x"], np.float32)
    B_, S_, D_ = x.shape
    assert (B_, S_, D_) == (8, T, D)

    wg = _tileize(np.ascontiguousarray(np.asarray(inp["L0_attn_a_Wg"]).T), NCH)
    wv = _tileize(np.ascontiguousarray(np.asarray(inp["L0_attn_a_Wv"]).T), NCH)
    wd = _tileize(np.ascontiguousarray(np.asarray(inp["L0_attn_a_Wd"]).T), NCH)
    wgate = _tileize(np.ascontiguousarray(np.asarray(inp["L0_attn_a_W_gate"]).T), NFC)
    wup = _tileize(np.ascontiguousarray(np.asarray(inp["L0_attn_a_W_up"]).T), NFC)
    wout_t = np.ascontiguousarray(np.asarray(inp["L0_attn_a_W_out"]).T)
    wout = np.ascontiguousarray(wout_t.reshape(NFC, P, D))
    wr = _tileize(np.ascontiguousarray(np.asarray(inp["L0_ffn_W_r"]).T), NCH)
    wk = _tileize(np.ascontiguousarray(np.asarray(inp["L0_ffn_W_k"]).T), NCH)
    wvf = _tileize(np.ascontiguousarray(np.asarray(inp["L0_ffn_W_v"]).T), NCH)
    bmat = _tileize(np.ascontiguousarray(np.asarray(inp["L0_attn_b_B"]).T), NCH)
    cmat = _tileize(np.ascontiguousarray(np.asarray(inp["L0_attn_b_C"]).T), NCH)
    amat = _tileize(np.ascontiguousarray(np.asarray(inp["L0_attn_b_A"]).T), NCH)
    vecs = _vecpack(dict(
        bg=np.asarray(inp["L0_attn_a_bg"], np.float32),
        bv=np.asarray(inp["L0_attn_a_bv"], np.float32),
        bd=np.asarray(inp["L0_attn_a_bd"], np.float32),
        rmw=np.asarray(inp["L0_attn_a_rms_mix_w"], np.float32),
        rfw=np.asarray(inp["L0_attn_a_rms_ffn_w"], np.float32),
        mur=np.asarray(inp["L0_ffn_mu_r"], np.float32),
        muk=np.asarray(inp["L0_ffn_mu_k"], np.float32),
        gam=np.asarray(inp["L0_attn_b_gamma"], np.float32),
        bet=np.asarray(inp["L0_attn_b_beta"], np.float32),
    ))

    shared = dict(wg=wg, wv=wv, wd=wd, wgate=wgate, wup=wup, wout=wout,
                  wr=wr, wk=wk, wvf=wvf, bmat=bmat, cmat=cmat, amat=amat,
                  vecs=vecs)
    in_maps = []
    for b in range(8):
        xb = np.ascontiguousarray(x[b].T.reshape(NCH, P, T))
        m = dict(shared)
        m["xin"] = xb
        in_maps.append(m)

    nc = build_nc()
    res = run_bass_kernel_spmd(nc, in_maps, core_ids=list(range(8)))
    outs = []
    for b in range(8):
        o = res.results[b]["xout"].reshape(D, T).T
        outs.append(o)
    return np.stack(outs, axis=0).astype(np.float32)


# version 16
# speedup vs baseline: 1.0122x; 1.0122x over previous
import numpy as np
import concourse.bass as bass
import concourse.mybir as mybir
from concourse.bass_utils import run_bass_kernel_spmd

F32 = mybir.dt.float32
BF16 = mybir.dt.bfloat16
AF = mybir.ActivationFunctionType
OP = mybir.AluOpType

P = 128
D = 1024
NCH = D // P
T = 2048
F = 3072
NFC = F // P
TT = 256
NT = T // TT
EPS_LN = 1e-5
EPS_RMS = 1e-6
NPHASE = 5
DETECT_RACES = False
DRAIN_ENGINES = ()



class Plan:
    def __init__(self):
        self.ops = []
        self.count = {}

    def sem(self, name):
        self.count.setdefault(name, 0)
        return name

    def add(self, eng, emit, waits=(), incs=()):
        for (s, v) in waits:
            assert v <= self.count.get(s, 0), (
                f"wait on future token {s}>={v} (planned {self.count.get(s,0)})")
        toks = []
        for (s, a) in incs:
            self.count[s] = self.count.get(s, 0) + a
            toks.append(self.count[s])
        self.ops.append((eng, emit, list(waits), list(incs)))
        return toks[0] if len(toks) == 1 else toks


ENGS = ("tensor", "vector", "scalar", "sync", "gpsimd")




def build_nc():
    nc = bass.Bass("TRN2", target_bir_lowering=False,
                   detect_race_conditions=DETECT_RACES)

    xin = nc.dram_tensor("xin", [NCH, P, T], F32, kind="ExternalInput").ap()
    xout = nc.dram_tensor("xout", [NCH, P, T], F32, kind="ExternalOutput").ap()
    wg = nc.dram_tensor("wg", [NCH, P, D], F32, kind="ExternalInput").ap()
    wv = nc.dram_tensor("wv", [NCH, P, D], F32, kind="ExternalInput").ap()
    wd = nc.dram_tensor("wd", [NCH, P, D], F32, kind="ExternalInput").ap()
    wgate = nc.dram_tensor("wgate", [NFC, P, D], F32, kind="ExternalInput").ap()
    wup = nc.dram_tensor("wup", [NFC, P, D], F32, kind="ExternalInput").ap()
    wout = nc.dram_tensor("wout", [NFC, P, D], F32, kind="ExternalInput").ap()
    wr = nc.dram_tensor("wr", [NCH, P, D], F32, kind="ExternalInput").ap()
    wk = nc.dram_tensor("wk", [NCH, P, D], F32, kind="ExternalInput").ap()
    wvf = nc.dram_tensor("wvf", [NCH, P, D], F32, kind="ExternalInput").ap()
    bmat = nc.dram_tensor("bmat", [NCH, P, D], F32, kind="ExternalInput").ap()
    cmat = nc.dram_tensor("cmat", [NCH, P, D], F32, kind="ExternalInput").ap()
    amat = nc.dram_tensor("amat", [NCH, P, D], F32, kind="ExternalInput").ap()
    vecs = nc.dram_tensor("vecs", [P, 80], F32, kind="ExternalInput").ap()

    import contextlib
    with contextlib.ExitStack() as S:
        acols = max(NCH * T, 8 * (TT + 1) + 8 + 5 * 8 * TT, 32 * TT)
        xsb = S.enter_context(nc.sbuf_tensor("xsb", [P, NCH * T], F32))
        arena = S.enter_context(nc.sbuf_tensor("arena", [P, acols], F32))
        prod = S.enter_context(nc.sbuf_tensor("prod", [P, NFC * TT], F32))
        ahi = S.enter_context(nc.sbuf_tensor("ahi", [P, NCH * D], BF16))
        alo = S.enter_context(nc.sbuf_tensor("alo", [P, NCH * D], BF16))
        wbuf0 = S.enter_context(nc.sbuf_tensor("wbuf0", [P, D], F32))
        wbuf1 = S.enter_context(nc.sbuf_tensor("wbuf1", [P, D], F32))
        sqb0 = S.enter_context(nc.sbuf_tensor("sqb0", [P, TT], F32))
        sqb1 = S.enter_context(nc.sbuf_tensor("sqb1", [P, TT], F32))
        vsb = S.enter_context(nc.sbuf_tensor("vsb", [P, 80], F32))
        ones = S.enter_context(nc.sbuf_tensor("ones", [P, 1], F32))
        onesk = S.enter_context(nc.sbuf_tensor("onesk", [1, P], F32))
        srow = S.enter_context(nc.sbuf_tensor("srow", [1, 8 * TT], F32))
        carry = S.enter_context(nc.sbuf_tensor("carry", [P, NCH], F32))
        hhl = S.enter_context(nc.sbuf_tensor("hhl", [P, 4 * NCH], BF16))
        hf = S.enter_context(nc.sbuf_tensor("hf", [P, NCH], F32))
        ztmp = S.enter_context(nc.sbuf_tensor("ztmp", [P, NCH], F32))
        ztmp2 = S.enter_context(nc.sbuf_tensor("ztmp2", [P, NCH], F32))
        evb = S.enter_context(nc.sbuf_tensor("evb", [P, 2 * TT], F32))
        junk = S.enter_context(nc.sbuf_tensor("junk", [P, 8], F32))
        ps = S.enter_context(nc.psum_tensor("ps", [P, 4096], F32))

        sems = {}
        for name in ("ld", "wd0", "wd1", "wd2", "wd3", "pe", "ve", "ac", "smm",
                     "szv", "sth", "sep", "sbf", "hcp", "st"):
            sems[name] = S.enter_context(nc.semaphore(name))

        plan = Plan()
        for n in sems:
            plan.sem(n)

        emit_program(plan, dict(
            xin=xin, xout=xout, wg=wg, wv=wv, wd=wd, wgate=wgate, wup=wup,
            wout=wout, wr=wr, wk=wk, wvf=wvf, bmat=bmat, cmat=cmat, amat=amat,
            vecs=vecs, xsb=xsb, arena=arena, prod=prod, ahi=ahi, alo=alo,
            wbuf=[wbuf0, wbuf1], sqb=[sqb0, sqb1], vsb=vsb, ones=ones,
            onesk=onesk, srow=srow, carry=carry, hhl=hhl, hf=hf, ztmp=ztmp,
            ztmp2=ztmp2, evb=evb, junk=junk, ps=ps))

        block = S.enter_context(nc.Block())

        def mk(eng_name):
            drainy = eng_name in DRAIN_ENGINES
            def body(e):
                first = True
                for (g, emit, waits, incs) in plan.ops:
                    if g != eng_name:
                        continue
                    for (s, v) in waits:
                        e.wait_ge(sems[s], v)
                    if drainy and not first:
                        e.drain()
                    first = False
                    inst = emit(e)
                    for (s, a) in incs:
                        inst.then_inc(sems[s], a)
            return body

        block.tensor(mk("tensor"))
        block.vector(mk("vector"))
        block.scalar(mk("scalar"))
        block.sync(mk("sync"))
        block.gpsimd(mk("gpsimd"))
    return nc


def emit_program(plan, t):
    xsb, arena, prod = t["xsb"], t["arena"], t["prod"]
    ahi, alo = t["ahi"], t["alo"]
    wbuf, sqb = t["wbuf"], t["sqb"]
    vsb, ones, onesk, srow = t["vsb"], t["ones"], t["onesk"], t["srow"]
    carry, hhl, hf = t["carry"], t["hhl"], t["hf"]
    ztmp, ztmp2, evb = t["ztmp"], t["ztmp2"], t["evb"]
    junk = t["junk"]

    def vdummy():
        plan.add("vector", lambda e: e.memset(junk[:, 0:8], 0.0))
    ps = t["ps"]

    NT_ = T // TT

    def bank(j, lo, n):
        return ps[:, j * 512 + lo: j * 512 + lo + n]

    def bankrow(j, lo, n):
        return ps[0:1, j * 512 + lo: j * 512 + lo + n]

    def bank_scan(j, n):
        return ps[:, j * 512: j * 512 + n]

    def xch(c, t0, n):
        return xsb[:, c * T + t0: c * T + t0 + n]

    def ach(c, t0, n):
        return arena[:, c * T + t0: c * T + t0 + n]

    state = {
        "wbuf_free": [[("ld", 0)], [("ld", 0)]],
        "last_sq_pe": 0,
        "stats_free": 0,
    }

    for c in range(NCH):
        plan.add("sync", (lambda e, c=c: e.dma_start(
            out=xch(c, 0, T), in_=t["xin"][c, :, :])), incs=[("ld", 16)])
    plan.add("sync", lambda e: e.dma_start(out=vsb[:, :], in_=t["vecs"][:, :]),
             incs=[("ld", 16)])
    ld_done = plan.count["ld"]
    plan.add("vector", lambda e: e.memset(ones[:, :], 1.0))
    plan.add("vector", lambda e: e.memset(onesk[:, :], 1.0))
    plan.add("vector", lambda e: e.memset(carry[:, :], 0.0))
    plan.add("vector", lambda e: e.memset(t["hhl"][:, :], 0.0),
             waits=[("ld", ld_done)], incs=[("ve", 1)])
    init_tok = plan.count["ve"]


    def stats_pass(src_ap_fn, tile0, n, eps, want_mean, wait_pairs):
        srcs = [src_ap_fn(c) for c in range(NCH)]
        sq_consumed = [state["last_sq_pe"], state["last_sq_pe"]]
        for c in range(NCH):
            sq = sqb[c % 2]
            w = list(wait_pairs) if c == 0 else []
            if sq_consumed[c % 2]:
                w.append(("pe", sq_consumed[c % 2]))
            plan.add("scalar", (lambda e, src=srcs[c], sq=sq: e.square(
                out=sq[:, 0:n], in_=src)), waits=w, incs=[("ac", 1)])
            ac_tok = plan.count["ac"]
            w2 = [("ac", ac_tok)]
            wfree = ([("ve", state["stats_free"])]
                     if (c == 0 and state["stats_free"]) else [])
            plan.add("tensor", (lambda e, c=c, src=srcs[c]: e.matmul(
                bankrow(0, 0, n), ones[:, :], src,
                start=(c == 0), stop=(c == NCH - 1))),
                waits=(list(wait_pairs) + wfree) if c == 0 else [])
            tok = plan.add("tensor", (lambda e, c=c, sq=sq: e.matmul(
                bankrow(1, 0, n), ones[:, :], sq[:, 0:n],
                start=(c == 0), stop=(c == NCH - 1))), waits=w2,
                incs=[("pe", 1)])
            sq_consumed[c % 2] = tok
        state["last_sq_pe"] = plan.count["pe"]
        pe_tok = plan.count["pe"]
        mu = srow[0:1, 0:n]
        var = srow[0:1, n:2 * n]
        rstd = srow[0:1, 2 * n:3 * n]
        tmp = srow[0:1, 3 * n:4 * n]
        plan.add("vector", lambda e: e.tensor_scalar(
            out=mu, in0=bankrow(0, 0, n), scalar1=1.0 / D, scalar2=None,
            op0=OP.mult), waits=[("pe", pe_tok)])
        plan.add("vector", lambda e: e.tensor_scalar(
            out=var, in0=bankrow(1, 0, n), scalar1=1.0 / D, scalar2=eps,
            op0=OP.mult, op1=OP.add))
        if want_mean:
            plan.add("vector", lambda e: e.tensor_tensor(
                out=tmp, in0=mu, in1=mu, op=OP.mult))
            plan.add("vector", lambda e: e.drain())
            plan.add("vector", lambda e: e.tensor_tensor(
                out=var, in0=var, in1=tmp, op=OP.subtract))
        plan.add("vector", lambda e: e.drain())
        plan.add("vector", lambda e: e.reciprocal(out=rstd, in_=var),
                 incs=[("ve", 1)])
        ve_tok = plan.count["ve"]
        plan.add("scalar", lambda e: e.sqrt(out=rstd, in_=rstd),
                 waits=[("ve", ve_tok)], incs=[("ac", 1)])
        ac_tok = plan.count["ac"]
        plan.add("vector", lambda e: e.tensor_tensor(
            out=tmp, in0=rstd, in1=rstd, op=OP.mult), waits=[("ac", ac_tok)])
        plan.add("vector", lambda e: e.drain())
        plan.add("vector", lambda e: e.tensor_tensor(
            out=tmp, in0=tmp, in1=var, op=OP.mult))
        plan.add("vector", lambda e: e.drain())
        plan.add("vector", lambda e: e.tensor_scalar(
            out=tmp, in0=tmp, scalar1=-0.5, scalar2=1.5, op0=OP.mult,
            op1=OP.add))
        plan.add("vector", lambda e: e.drain())
        plan.add("vector", lambda e: e.tensor_tensor(
            out=rstd, in0=rstd, in1=tmp, op=OP.mult), incs=[("ve", 1)])
        ve2 = plan.count["ve"]
        if want_mean:
            plan.add("vector", lambda e: e.drain())
            plan.add("vector", lambda e: e.tensor_tensor(
                out=tmp, in0=mu, in1=rstd, op=OP.mult))
            plan.add("vector", lambda e: e.drain())
            plan.add("vector", lambda e: e.tensor_scalar(
                out=tmp, in0=tmp, scalar1=-1.0, scalar2=None, op0=OP.mult),
                incs=[("ve", 1)])
            ve2 = plan.count["ve"]
        plan.add("tensor", lambda e: e.matmul(
            bank(1, 0, n), onesk[:, :], rstd, start=True,
            stop=not want_mean), waits=[("ve", ve2)],
            incs=([("pe", 1)] if not want_mean else []))
        if want_mean:
            plan.add("tensor", lambda e: e.matmul(
                bank(1, n, n), onesk[:, :], tmp, start=False, stop=True),
                incs=[("pe", 1)])
        return ("pe", plan.count["pe"])

    def stream_w(dram, chunk, which, wait_pairs):
        sem = "wd%d" % which
        eng = "sync" if which == 0 else "gpsimd"
        tok = plan.add(eng, (lambda e, dram=dram, chunk=chunk, which=which:
                             e.dma_start(out=wbuf[which][:, :],
                                         in_=dram[chunk, :, :])),
                       waits=list(wait_pairs) + [(sem, plan.count[sem])],
                       incs=[(sem, 16)])
        return [(sem, tok)]

    def mm_group(out_ap, stat_sb, moving_fn, wait_pairs, extra_last=()):
        movings = [moving_fn(c) for c in range(NCH)]
        for c in range(NCH):
            w = list(wait_pairs) if c == 0 else []
            plan.add("tensor", (lambda e, c=c, m=movings[c]: e.matmul(
                out_ap, stat_sb[:, c * P:(c + 1) * P], m,
                start=(c == 0), stop=(c == NCH - 1))),
                waits=w, incs=([("pe", 1)] if c == NCH - 1 else []))
        return ("pe", plan.count["pe"])

    ph1_wait = [("ld", ld_done), ("ve", init_tok)]

    evb_free = [("ld", 0), ("ld", 0)]
    ln_done_tok = {}

    for ti in range(NT_):
        t0 = ti * TT
        w0 = ph1_wait if ti == 0 else []

        btok = stats_pass(lambda c: xch(c, t0, TT), t0, TT, EPS_LN, True, w0)

        for c in range(NCH):
            plan.add("vector", (lambda e, o=ach(c, t0, TT),
                                i0=xch(c, t0, TT): e.tensor_tensor(
                out=o, in0=i0, in1=bank(1, 0, TT),
                op=OP.mult)), waits=[btok] if c == 0 else [])
        for c in range(NCH):
            plan.add("vector", (lambda e, o=ach(c, t0, TT): e.tensor_tensor(
                out=o, in0=o, in1=bank(1, TT, TT),
                op=OP.add)))
        plan.add("vector", lambda e: e.drain(), incs=[("ve", 1)])
        ln_tok = ("ve", plan.count["ve"])
        state["stats_free"] = plan.count["ve"]

        btok = stats_pass(lambda c: ach(c, t0, TT), t0, TT, EPS_RMS, False,
                          [ln_tok])
        for c in range(NCH):
            plan.add("vector", (lambda e, c=c, i0=ach(c, t0, TT):
                                e.tensor_tensor(
                out=prod[:, c * TT:(c + 1) * TT], in0=i0,
                in1=bank(1, 0, TT), op=OP.mult)),
                waits=[btok] if c == 0 else [])
        for c in range(NCH):
            plan.add("vector", (lambda e, c=c: e.tensor_scalar(
                out=prod[:, c * TT:(c + 1) * TT],
                in0=prod[:, c * TT:(c + 1) * TT],
                scalar1=vsb[:, 24 + c:25 + c], scalar2=None, op0=OP.mult)))
        plan.add("vector", lambda e: e.drain(), incs=[("ve", 1)])
        xn_tok = ("ve", plan.count["ve"])
        state["stats_free"] = plan.count["ve"]

        if ti == 0:
            gvd_prev = [0, 0, 0]
            xs_prev = 0
        for j in range(NCH):
            for m, dram in enumerate((t["wg"], t["wv"], t["wd"])):
                which = (j * 3 + m) % 2
                wtok = stream_w(dram, j, which, state["wbuf_free"][which])
                bk = 2 + m
                mmtok = mm_group(
                    bank(bk, 0, TT), wbuf[which],
                    (lambda c, j=j: prod[:, c * TT:(c + 1) * TT]),
                    wtok + [xn_tok] + ([("ac", gvd_prev[m])]
                                      if gvd_prev[m] else []))
                state["wbuf_free"][which] = [mmtok]
                ev_guard = [("ve", xs_prev)] if (m < 2 and xs_prev) else []
                if m == 0:
                    plan.add("scalar", (lambda e, j=j: e.activation(
                        out=evb[:, 0:TT], in_=bank(2, 0, TT), func=AF.Sigmoid,
                        bias=vsb[:, 0 + j:1 + j], scale=1.0)),
                        waits=[mmtok] + ev_guard, incs=[("ac", 1)])
                elif m == 1:
                    plan.add("scalar", (lambda e, j=j: e.activation(
                        out=evb[:, TT:2 * TT], in_=bank(3, 0, TT),
                        func=AF.Tanh, bias=vsb[:, 8 + j:9 + j], scale=1.0)),
                        waits=[mmtok] + ev_guard, incs=[("ac", 1)])
                else:
                    plan.add("scalar", (lambda e, j=j: e.activation(
                        out=prod[:, (16 + j) * TT:(17 + j) * TT],
                        in_=bank(4, 0, TT), func=AF.Sigmoid,
                        bias=vsb[:, 16 + j:17 + j], scale=1.0)),
                        waits=[mmtok], incs=[("ac", 1)])
                gvd_prev[m] = plan.count["ac"]
            ac_j = plan.count["ac"]
            xs_prev = plan.add("vector", (lambda e, j=j: e.tensor_tensor(
                out=prod[:, (8 + j) * TT:(9 + j) * TT], in0=evb[:, 0:TT],
                in1=evb[:, TT:2 * TT], op=OP.mult)),
                waits=[("ac", ac_j)], incs=[("ve", 1)])
            plan.add("vector", (lambda e, j=j: e.tensor_scalar(
                out=prod[:, (16 + j) * TT:(17 + j) * TT],
                in0=prod[:, (16 + j) * TT:(17 + j) * TT],
                scalar1=0.998, scalar2=0.001, op0=OP.mult, op1=OP.add)),
                incs=[("ve", 1)])
        for c in range(NCH):
            plan.add("vector", (lambda e, c=c: e.tensor_tensor_scan(
                out=evb[:, 0:TT], data0=prod[:, (16 + c) * TT:(17 + c) * TT],
                data1=prod[:, (8 + c) * TT:(9 + c) * TT],
                initial=carry[:, c:c + 1], op0=OP.mult, op1=OP.add)))
            plan.add("vector", lambda e: e.drain())
            plan.add("vector", (lambda e, c=c: e.tensor_copy(
                out=carry[:, c:c + 1], in_=evb[:, TT - 1:TT])))
            plan.add("vector", (lambda e, o=ach(c, t0, TT): e.tensor_tensor(
                out=o, in0=o, in1=evb[:, 0:TT],
                op=OP.add)))
            plan.add("vector", lambda e: e.drain())
            plan.add("vector", (lambda e, o=xch(c, t0, TT),
                                i1=ach(c, t0, TT): e.tensor_tensor(
                out=o, in0=o, in1=i1,
                op=OP.add)))
        plan.add("vector", lambda e: e.drain(), incs=[("ve", 1)])
        ph1_wait = [("ve", plan.count["ve"])]

    mix_done = [("ve", plan.count["ve"])]
    if NPHASE == 1:
        return finish(plan, t, xch, plan.count["ve"])

    for ti in range(NT_):
        t0 = ti * TT
        w0 = mix_done if ti == 0 else []
        btok = stats_pass(lambda c: ach(c, t0, TT), t0, TT, EPS_RMS, False, w0)
        for c in range(NCH):
            plan.add("vector", (lambda e, o=ach(c, t0, TT): e.tensor_tensor(
                out=o, in0=o, in1=bank(1, 0, TT),
                op=OP.mult)), waits=[btok] if c == 0 else [])
        for c in range(NCH):
            plan.add("vector", (lambda e, c=c, o=ach(c, t0, TT):
                                e.tensor_scalar(
                out=o, in0=o,
                scalar1=vsb[:, 32 + c:33 + c], scalar2=None, op0=OP.mult)))
        plan.add("vector", lambda e: e.drain(), incs=[("ve", 1)])
        xn2_tok = ("ve", plan.count["ve"])
        state["stats_free"] = plan.count["ve"]

        if ti == 0:
            silu_prev = None
            prod_prev = None
            evac_prev = None
        for fj in range(NFC):
            wt_g = stream_w(t["wgate"], fj, 0, state["wbuf_free"][0])
            tok_g = mm_group(bank(2, 0, TT), wbuf[0],
                             (lambda c: ach(c, t0, TT)),
                             wt_g + [xn2_tok] +
                             ([silu_prev] if silu_prev else []))
            state["wbuf_free"][0] = [tok_g]
            plan.add("scalar", (lambda e: e.activation(
                out=evb[:, 0:TT], in_=bank(2, 0, TT), func=AF.Silu)),
                waits=[tok_g] + ([prod_prev] if prod_prev else []),
                incs=[("ac", 1)])
            silu_prev = ("ac", plan.count["ac"])
            wt_u = stream_w(t["wup"], fj, 1, state["wbuf_free"][1])
            tok_u = mm_group(bank(3, 0, TT), wbuf[1],
                             (lambda c: ach(c, t0, TT)),
                             wt_u + ([prod_prev] if prod_prev else []))
            state["wbuf_free"][1] = [tok_u]
            plan.add("vector", (lambda e, fj=fj: e.tensor_tensor(
                out=prod[:, fj * TT:(fj + 1) * TT], in0=evb[:, 0:TT],
                in1=bank(3, 0, TT), op=OP.mult)),
                waits=[tok_u, silu_prev], incs=[("ve", 1)])
            prod_prev = ("ve", plan.count["ve"])

        for half in range(2):
            for fj in range(NFC):
                which = fj % 2
                sem_o = "wd%d" % which
                wt_o = plan.add("sync" if which == 0 else "gpsimd",
                                (lambda e, fj=fj, half=half,
                                         which=which: e.dma_start(
                    out=wbuf[which][:, 0:512],
                    in_=t["wout"][fj, :, half * 512:(half + 1) * 512])),
                    waits=state["wbuf_free"][which] +
                    [(sem_o, plan.count[sem_o])],
                    incs=[(sem_o, 16)])
                for dc in range(4):
                    w = []
                    if dc == 0:
                        w.append(("wd%d" % which, wt_o))
                    if fj == 0 and dc == 0:
                        w.append(prod_prev)
                        if evac_prev is not None:
                            w.append(evac_prev)
                    tok = plan.add("tensor", (lambda e, fj=fj, dc=dc,
                                              which=which: e.matmul(
                        bank(4 + dc, 0, TT),
                        wbuf[which][:, dc * P:(dc + 1) * P],
                        prod[:, fj * TT:(fj + 1) * TT],
                        start=(fj == 0), stop=(fj == NFC - 1))),
                        waits=w,
                        incs=([("pe", 1)] if dc == 3 else []))
                state["wbuf_free"][which] = [("pe", plan.count["pe"])]
            accum_tok = ("pe", plan.count["pe"])
            for dc in range(4):
                c = half * 4 + dc
                plan.add("vector", (lambda e, dc=dc, o=xch(c, t0, TT):
                                    e.tensor_tensor(
                    out=o, in0=o,
                    in1=bank(4 + dc, 0, TT), op=OP.add)),
                    waits=[accum_tok] if dc == 0 else [])
            plan.add("vector", lambda e: e.drain(), incs=[("ve", 1)])
            evac_prev = ("ve", plan.count["ve"])

    ffn1_done = [("ve", plan.count["ve"])]
    if NPHASE == 2:
        return finish(plan, t, xch, plan.count["ve"])

    def emit_ffn(begin_wait):
        LNB = 0
        XR = 2056 + 8
        XK = XR + 8 * TT
        TMP = XK + 8 * TT
        RB = TMP + 8 * TT
        KA = RB + 8 * TT
        last = KA + 8 * TT

        def lnb(c, o, n):
            return arena[:, LNB + c * (TT + 1) + o: LNB + c * (TT + 1) + o + n]

        def ar(base, c, n=TT):
            return arena[:, base + c * TT: base + c * TT + n]

        wv = begin_wait
        for ti in range(NT_):
            t0 = ti * TT
            w0 = wv if ti == 0 else []
            for c in range(NCH):
                if ti == 0:
                    plan.add("vector", (lambda e, c=c: e.memset(
                        lnb(c, 0, 1), 0.0)), waits=w0 if c == 0 else [])
                else:
                    plan.add("vector", (lambda e, c=c: e.tensor_copy(
                        out=lnb(c, 0, 1), in_=lnb(c, TT, 1))))
            plan.add("vector", lambda e: e.drain(), incs=[("ve", 1)])
            sv_tok = ("ve", plan.count["ve"])
            btok = stats_pass(lambda c: xch(c, t0, TT), t0, TT, EPS_LN, True,
                              [sv_tok])
            for c in range(NCH):
                plan.add("vector", (lambda e, c=c, i0=xch(c, t0, TT):
                                    e.tensor_tensor(
                    out=lnb(c, 1, TT), in0=i0,
                    in1=bank(1, 0, TT), op=OP.mult)),
                    waits=[btok] if c == 0 else [])
            for c in range(NCH):
                plan.add("vector", (lambda e, c=c: e.tensor_tensor(
                    out=lnb(c, 1, TT), in0=lnb(c, 1, TT), in1=bank(1, TT, TT),
                    op=OP.add)))
            for c in range(NCH):
                plan.add("vector", (lambda e, c=c: e.tensor_tensor(
                    out=ar(TMP, c), in0=lnb(c, 1, TT), in1=lnb(c, 0, TT),
                    op=OP.subtract)))
            for c in range(NCH):
                plan.add("vector", (lambda e, c=c: e.scalar_tensor_tensor(
                    out=ar(XR, c), in0=ar(TMP, c), scalar=vsb[:, 40 + c:41 + c],
                    in1=lnb(c, 0, TT), op0=OP.mult, op1=OP.add)))
            for c in range(NCH):
                plan.add("vector", (lambda e, c=c: e.scalar_tensor_tensor(
                    out=ar(XK, c), in0=ar(TMP, c), scalar=vsb[:, 48 + c:49 + c],
                    in1=lnb(c, 0, TT), op0=OP.mult, op1=OP.add)))
            plan.add("vector", lambda e: e.drain(), incs=[("ve", 1)])
            xrk_tok = ("ve", plan.count["ve"])
            state["stats_free"] = plan.count["ve"]

            for j in range(NCH):
                wt_r = stream_w(t["wr"], j, 0, state["wbuf_free"][0])
                tok_r = mm_group(bank(2, 0, TT), wbuf[0],
                                 (lambda c: ar(XR, c)), wt_r + [xrk_tok] +
                                 ([] if (ti == 0 and j == 0) else [r_prev]))
                state["wbuf_free"][0] = [tok_r]
                plan.add("scalar", (lambda e, j=j: e.activation(
                    out=ar(RB, j), in_=bank(2, 0, TT), func=AF.Sigmoid)),
                    waits=[tok_r], incs=[("ac", 1)])
                r_prev = ("ac", plan.count["ac"])
                wt_k = stream_w(t["wk"], j, 1, state["wbuf_free"][1])
                tok_k = mm_group(bank(3, 0, TT), wbuf[1],
                                 (lambda c: ar(XK, c)), wt_k +
                                 ([] if (ti == 0 and j == 0) else [k_prev]))
                state["wbuf_free"][1] = [tok_k]
                plan.add("scalar", (lambda e, j=j: e.activation(
                    out=evb[:, 0:TT], in_=bank(3, 0, TT), func=AF.Relu)),
                    waits=[tok_k], incs=[("ac", 1)])
                plan.add("scalar", (lambda e, j=j: e.square(
                    out=ar(KA, j), in_=evb[:, 0:TT])), incs=[("ac", 1)])
                k_prev = ("ac", plan.count["ac"])
            ka_tok = k_prev

            for j in range(NCH):
                wt_v = stream_w(t["wvf"], j, j % 2, state["wbuf_free"][j % 2])
                tok_v = mm_group(bank(2 + (j % 2), 0, TT), wbuf[j % 2],
                                 (lambda c: ar(KA, c)), wt_v + [ka_tok] +
                                 ([] if (ti == 0 and j == 0) else [o_prev]))
                state["wbuf_free"][j % 2] = [tok_v]
                plan.add("vector", (lambda e, j=j: e.tensor_tensor(
                    out=ar(TMP, j), in0=ar(RB, j),
                    in1=bank(2 + (j % 2), 0, TT), op=OP.mult)),
                    waits=[tok_v], incs=[("ve", 1)])
                plan.add("vector", lambda e: e.drain())
                plan.add("vector", (lambda e, j=j, o=xch(j, t0, TT):
                                    e.tensor_tensor(
                    out=o, in0=o,
                    in1=ar(TMP, j), op=OP.add)), incs=[("ve", 1)])
                o_prev = ("ve", plan.count["ve"])
            wv = [o_prev]
        return [("ve", plan.count["ve"])]

    ffn2_done = emit_ffn(ffn1_done)
    if NPHASE == 3:
        return finish(plan, t, xch, plan.count["ve"])

    plan.add("sync", lambda e: e.nop(),
             waits=[("wd%d" % k, plan.count["wd%d" % k]) for k in range(4)])
    for g in range(2):
        for c in range(4):
            ch = g * 4 + c
            tok = plan.add("sync", (lambda e, ch=ch, c=c: e.dma_start(
                out=prod[:, c * D:(c + 1) * D], in_=t["amat"][ch, :, :])),
                waits=(list(ffn2_done) if (g == 0 and c == 0) else
                       [sp_tok] if c == 0 else []) +
                [("wd0", plan.count["wd0"])], incs=[("wd0", 16)])
        wd_tok = plan.count["wd0"]
        for c in range(4):
            ch = g * 4 + c
            plan.add("vector", (lambda e, ch=ch, c=c: e.tensor_copy(
                out=ahi[:, ch * D:(ch + 1) * D],
                in_=prod[:, c * D:(c + 1) * D])),
                waits=[("wd0", wd_tok)] if c == 0 else [])
        for c in range(4):
            ch = g * 4 + c
            plan.add("vector", (lambda e, ch=ch, c=c: e.tensor_tensor(
                out=alo[:, ch * D:(ch + 1) * D],
                in0=prod[:, c * D:(c + 1) * D],
                in1=ahi[:, ch * D:(ch + 1) * D], op=OP.subtract)))
        plan.add("vector", lambda e: e.drain(), incs=[("ve", 1)])
        sp_tok = ("ve", plan.count["ve"])
    a_ready = sp_tok

    for j in range(6):
        plan.add("sync", (lambda e, j=j: e.dma_start(
            out=prod[:, j * D:(j + 1) * D], in_=t["cmat"][j, :, :])),
            waits=([a_ready, ("wd0", plan.count["wd0"])] if j == 0 else []),
            incs=[("wd0", 16)])
    for j in range(6, NCH):
        plan.add("sync", (lambda e, j=j: e.dma_start(
            out=arena[:, 10240 + (j - 6) * D:10240 + (j - 5) * D],
            in_=t["cmat"][j, :, :])), incs=[("wd0", 16)])
    bc_tok = ("wd0", plan.count["wd0"])

    def cstat(j):
        if j < 6:
            return prod[:, j * D:(j + 1) * D]
        return arena[:, 10240 + (j - 6) * D:10240 + (j - 5) * D]

    XLN = 0
    XB = 8 * TT
    HB = 16 * TT
    PRE = 24 * TT

    def arn(base, c, n=TT):
        return arena[:, base + c * TT: base + c * TT + n]

    gs = 0
    ph3_wait = [a_ready]
    ys_prev = None
    for ti in range(NT_):
        t0 = ti * TT
        w0 = ph3_wait if ti == 0 else []
        btok = stats_pass(lambda c: xch(c, t0, TT), t0, TT, EPS_LN, True, w0)
        for c in range(NCH):
            plan.add("vector", (lambda e, c=c, i0=xch(c, t0, TT):
                                e.tensor_tensor(
                out=arn(XLN, c), in0=i0, in1=bank(1, 0, TT),
                op=OP.mult)), waits=[btok] if c == 0 else [])
        for c in range(NCH):
            plan.add("vector", (lambda e, c=c: e.tensor_tensor(
                out=arn(XLN, c), in0=arn(XLN, c), in1=bank(1, TT, TT),
                op=OP.add)))
        plan.add("vector", lambda e: e.drain(), incs=[("ve", 1)])
        xln_tok = ("ve", plan.count["ve"])
        state["stats_free"] = plan.count["ve"]

        for j in range(NCH):
            wt_b = stream_w(t["bmat"], j, j % 2,
                            state["wbuf_free"][j % 2])
            tok_b = mm_group(bank(2 + (j % 2), 0, TT), wbuf[j % 2],
                             (lambda c: arn(XLN, c)),
                             wt_b + [xln_tok] +
                             ([bc_tok] if (ti == 0 and j == 0) else []) +
                             ([] if (ti == 0 and j <= 1) else [xb_prev]))
            state["wbuf_free"][j % 2] = [tok_b]
            plan.add("vector", (lambda e, j=j: e.tensor_copy(
                out=arn(XB, j), in_=bank(2 + (j % 2), 0, TT))),
                waits=[tok_b], incs=[("ve", 1)])
            xb_prev = ("ve", plan.count["ve"])
        xb_tok = xb_prev

        for s in range(TT):
            br = (gs % 2) * 2 * NCH
            bw = ((gs + 1) % 2) * 2 * NCH

            def scan_mm(j, i, w, start, stop, smm_inc):
                til = j * NCH + i
                plan.add("tensor", (lambda e, til=til, br=br, i=i,
                                    start=start: e.matmul(
                    bank_scan(til // NCH, 2),
                    ahi[:, til * P:(til + 1) * P],
                    hhl[:, br + i * 2:br + i * 2 + 2],
                    start=start, stop=False)),
                    waits=w)
                plan.add("tensor", (lambda e, til=til, br=br, i=i,
                                    stop=stop: e.matmul(
                    bank_scan(til // NCH, 1),
                    alo[:, til * P:(til + 1) * P],
                    hhl[:, br + i * 2:br + i * 2 + 1], start=False,
                    stop=stop)),
                    incs=([("smm", 1)] if smm_inc else []))

            for j in range(4, NCH):
                for i in range(4, NCH):
                    if j == 4 and i == 4 and gs > 0:
                        w0_ = [("sth", 2 * gs - 1)]
                        if s == 0:
                            w0_.append(xb_tok)
                        til = 4 * NCH + 4
                        plan.add("tensor", (lambda e, til=til, br=br:
                                 e.matmul(
                            bank_scan(4, 1),
                            ahi[:, til * P:(til + 1) * P],
                            hhl[:, br + 8:br + 9], start=True, stop=False)),
                            waits=w0_)
                        plan.add("tensor", (lambda e, til=til, br=br:
                                 e.matmul(
                            bank_scan(4, 1),
                            alo[:, til * P:(til + 1) * P],
                            hhl[:, br + 8:br + 9], start=False, stop=False)))
                        plan.add("tensor", (lambda e, til=til, br=br:
                                 e.matmul(
                            ps[:, 4 * 512 + 1:4 * 512 + 2],
                            ahi[:, til * P:(til + 1) * P],
                            hhl[:, br + 9:br + 10], start=False, stop=False)),
                            waits=[("sep", 2 * gs - 1)])
                        continue
                    w = []
                    if j == 4 and i == 4:
                        w = [xb_tok, a_ready]
                    scan_mm(j, i, w, start=(i == 4), stop=False, smm_inc=False)
            for j in range(4, NCH):
                for i in range(4):
                    w = [("sep", 2 * gs)] if (j == 4 and i == 0 and gs > 0) \
                        else []
                    scan_mm(j, i, w, start=False, stop=(i == 3),
                            smm_inc=(j == NCH - 1 and i == 3))
            for j in range(4):
                for i in range(4, NCH):
                    w = [("sbf", gs)] if (j == 0 and i == 4 and gs > 0) else []
                    scan_mm(j, i, w, start=(i == 4), stop=False, smm_inc=False)
            for j in range(4):
                for i in range(4):
                    scan_mm(j, i, [], start=False, stop=(i == 3),
                            smm_inc=(j == 3 and i == 3))

            hap = [arena[:, HB + 4 * h * TT + s: HB + (4 * h + 3) * TT
                          + s + 1:TT] for h in range(2)]
            for h in (1, 0):
                plan.add("vector", (lambda e, h=h, s=s: e.tensor_tensor(
                    out=ztmp[:, 4 * h:4 * h + 4],
                    in_=None, in0=ps[:, 2048 * h + 1:2048 * h + 1538:512],
                    in1=arena[:, XB + 4 * h * TT + s: XB + (4 * h + 3) * TT
                              + s + 1:TT],
                    op=OP.add) if False else e.tensor_tensor(
                    out=ztmp[:, 4 * h:4 * h + 4],
                    in0=ps[:, 2048 * h + 1:2048 * h + 1538:512],
                    in1=arena[:, XB + 4 * h * TT + s: XB + (4 * h + 3) * TT
                              + s + 1:TT],
                    op=OP.add)),
                    waits=[("smm", 2 * gs + (1 if h == 1 else 2))])
                vdummy()
                plan.add("vector", (lambda e, h=h: e.tensor_tensor(
                    out=ztmp2[:, 4 * h:4 * h + 4],
                    in0=ps[:, 2048 * h:2048 * h + 1537:512],
                    in1=ztmp[:, 4 * h:4 * h + 4],
                    op=OP.add)),
                    incs=[("szv", 1)])
                szv_v = plan.count["szv"]
                plan.add("scalar", (lambda e, h=h: e.activation(
                    out=ztmp2[:, 4 * h:4 * h + 4],
                    in_=ztmp2[:, 4 * h:4 * h + 4],
                    func=AF.Tanh)), waits=[("szv", szv_v)],
                    incs=([("sbf", 1)] if h == 0 else []))
                sth_v = plan.add("scalar", (lambda e, h=h, bw=bw: e.copy(
                    out=hhl[:, bw + 8 * h:bw + 8 * h + 7:2],
                    in_=ztmp2[:, 4 * h:4 * h + 4])),
                    incs=[("sth", 1)])
                plan.add("vector", (lambda e, h=h, bw=bw: e.tensor_tensor(
                    out=hhl[:, bw + 8 * h + 1:bw + 8 * h + 8:2],
                    in0=ztmp2[:, 4 * h:4 * h + 4],
                    in1=hhl[:, bw + 8 * h:bw + 8 * h + 7:2],
                    op=OP.subtract)),
                    waits=[("sth", sth_v)], incs=[("sep", 1)])
                plan.add("vector", (lambda e, h=h, ho=hap[h]: e.tensor_copy(
                    out=ho, in_=ztmp2[:, 4 * h:4 * h + 4])),
                    incs=[("hcp", 1)])
            gs += 1
        scan_tok = ("sep", plan.count["sep"])
        hcp_tok = ("hcp", plan.count["hcp"])

        for j in range(NCH):
            tok_c = mm_group(bank(2 + (j % 2), 0, TT), cstat(j),
                             (lambda c: arn(HB, c)), [scan_tok, hcp_tok] +
                             ([] if ys_prev is None else [ys_prev]))
            plan.add("vector", (lambda e, j=j: e.tensor_tensor(
                out=arn(PRE, j), in0=arn(XLN, j),
                in1=bank(2 + (j % 2), 0, TT), op=OP.add)),
                waits=[tok_c], incs=[("ve", 1)])
            ys_prev = ("ve", plan.count["ve"])

        btok = stats_pass(lambda c: arn(PRE, c), t0, TT, EPS_LN, True,
                          [ys_prev])
        T3T = 32 * TT
        for c in range(NCH):
            plan.add("vector", (lambda e, c=c: e.tensor_tensor(
                out=arn(T3T, c), in0=arn(PRE, c), in1=bank(1, 0, TT),
                op=OP.mult)), waits=[btok] if c == 0 else [])
        for c in range(NCH):
            plan.add("vector", (lambda e, c=c: e.tensor_tensor(
                out=arn(T3T, c), in0=arn(T3T, c), in1=bank(1, TT, TT),
                op=OP.add)))
        for c in range(NCH):
            plan.add("vector", (lambda e, c=c: e.tensor_scalar(
                out=arn(T3T, c), in0=arn(T3T, c),
                scalar1=vsb[:, 56 + c:57 + c], scalar2=vsb[:, 64 + c:65 + c],
                op0=OP.mult, op1=OP.add)))
        for c in range(NCH):
            plan.add("vector", (lambda e, c=c, o=xch(c, t0, TT):
                                e.tensor_tensor(
                out=o, in0=o, in1=arn(T3T, c),
                op=OP.add)))
        plan.add("vector", lambda e: e.drain(), incs=[("ve", 1)])
        ph3_wait = [("ve", plan.count["ve"])]
        state["stats_free"] = plan.count["ve"]

    attnb_done = [("ve", plan.count["ve"])]
    if NPHASE == 4:
        return finish(plan, t, xch, plan.count["ve"])

    emit_ffn(attnb_done)
    final_tok = plan.count["ve"]

    finish(plan, t, xch, final_tok)


def finish(plan, t, xch, final_tok):
    for c in range(NCH):
        plan.add("sync", (lambda e, c=c: e.dma_start(
            out=t["xout"][c, :, :], in_=xch(c, 0, T))),
            waits=[("ve", final_tok)] if c == 0 else [], incs=[("st", 16)])
    st_done = plan.count["st"]
    plan.add("sync", lambda e: e.nop(), waits=[("st", st_done)])




def _tileize(wt, n_out_chunks):
    K, M = wt.shape
    nk = K // P
    arr = wt.reshape(nk, P, n_out_chunks, P).transpose(2, 1, 0, 3)
    return np.ascontiguousarray(arr.reshape(n_out_chunks, P, K))


def _vecpack(d):
    v = np.zeros((P, 80), np.float32)
    names = ["bg", "bv", "bd", "rmw", "rfw", "mur", "muk", "gam", "bet"]
    for i, n in enumerate(names):
        v[:, i * 8:(i + 1) * 8] = d[n].reshape(NCH, P).T
    return v


def kernel(**inp):
    x = np.asarray(inp["L0_attn_a_x"], np.float32)
    B_, S_, D_ = x.shape
    assert (B_, S_, D_) == (8, T, D)

    wg = _tileize(np.ascontiguousarray(np.asarray(inp["L0_attn_a_Wg"]).T), NCH)
    wv = _tileize(np.ascontiguousarray(np.asarray(inp["L0_attn_a_Wv"]).T), NCH)
    wd = _tileize(np.ascontiguousarray(np.asarray(inp["L0_attn_a_Wd"]).T), NCH)
    wgate = _tileize(np.ascontiguousarray(np.asarray(inp["L0_attn_a_W_gate"]).T), NFC)
    wup = _tileize(np.ascontiguousarray(np.asarray(inp["L0_attn_a_W_up"]).T), NFC)
    wout_t = np.ascontiguousarray(np.asarray(inp["L0_attn_a_W_out"]).T)
    wout = np.ascontiguousarray(wout_t.reshape(NFC, P, D))
    wr = _tileize(np.ascontiguousarray(np.asarray(inp["L0_ffn_W_r"]).T), NCH)
    wk = _tileize(np.ascontiguousarray(np.asarray(inp["L0_ffn_W_k"]).T), NCH)
    wvf = _tileize(np.ascontiguousarray(np.asarray(inp["L0_ffn_W_v"]).T), NCH)
    bmat = _tileize(np.ascontiguousarray(np.asarray(inp["L0_attn_b_B"]).T), NCH)
    cmat = _tileize(np.ascontiguousarray(np.asarray(inp["L0_attn_b_C"]).T), NCH)
    amat = _tileize(np.ascontiguousarray(np.asarray(inp["L0_attn_b_A"]).T), NCH)
    vecs = _vecpack(dict(
        bg=np.asarray(inp["L0_attn_a_bg"], np.float32),
        bv=np.asarray(inp["L0_attn_a_bv"], np.float32),
        bd=np.asarray(inp["L0_attn_a_bd"], np.float32),
        rmw=np.asarray(inp["L0_attn_a_rms_mix_w"], np.float32),
        rfw=np.asarray(inp["L0_attn_a_rms_ffn_w"], np.float32),
        mur=np.asarray(inp["L0_ffn_mu_r"], np.float32),
        muk=np.asarray(inp["L0_ffn_mu_k"], np.float32),
        gam=np.asarray(inp["L0_attn_b_gamma"], np.float32),
        bet=np.asarray(inp["L0_attn_b_beta"], np.float32),
    ))

    shared = dict(wg=wg, wv=wv, wd=wd, wgate=wgate, wup=wup, wout=wout,
                  wr=wr, wk=wk, wvf=wvf, bmat=bmat, cmat=cmat, amat=amat,
                  vecs=vecs)
    in_maps = []
    for b in range(8):
        xb = np.ascontiguousarray(x[b].T.reshape(NCH, P, T))
        m = dict(shared)
        m["xin"] = xb
        in_maps.append(m)

    nc = build_nc()
    res = run_bass_kernel_spmd(nc, in_maps, core_ids=list(range(8)))
    outs = []
    for b in range(8):
        o = res.results[b]["xout"].reshape(D, T).T
        outs.append(o)
    return np.stack(outs, axis=0).astype(np.float32)


# version 20 (final)
# speedup vs baseline: 1.0709x; 1.0580x over previous
import numpy as np
import concourse.bass as bass
import concourse.mybir as mybir
from concourse.bass_utils import run_bass_kernel_spmd

F32 = mybir.dt.float32
BF16 = mybir.dt.bfloat16
AF = mybir.ActivationFunctionType
OP = mybir.AluOpType

P = 128
D = 1024
NCH = D // P
T = 2048
F = 3072
NFC = F // P
TT = 256
NT = T // TT
EPS_LN = 1e-5
EPS_RMS = 1e-6
NPHASE = 5
DETECT_RACES = False
DRAIN_ENGINES = ()



class Plan:
    def __init__(self):
        self.ops = []
        self.count = {}

    def sem(self, name):
        self.count.setdefault(name, 0)
        return name

    def add(self, eng, emit, waits=(), incs=()):
        for (s, v) in waits:
            assert v <= self.count.get(s, 0), (
                f"wait on future token {s}>={v} (planned {self.count.get(s,0)})")
        toks = []
        for (s, a) in incs:
            self.count[s] = self.count.get(s, 0) + a
            toks.append(self.count[s])
        self.ops.append((eng, emit, list(waits), list(incs)))
        return toks[0] if len(toks) == 1 else toks


ENGS = ("tensor", "vector", "scalar", "sync", "gpsimd")




def build_nc():
    nc = bass.Bass("TRN2", target_bir_lowering=False,
                   detect_race_conditions=DETECT_RACES)

    xin = nc.dram_tensor("xin", [NCH, P, T], F32, kind="ExternalInput").ap()
    xout = nc.dram_tensor("xout", [NCH, P, T], F32, kind="ExternalOutput").ap()
    wg = nc.dram_tensor("wg", [NCH, P, D], F32, kind="ExternalInput").ap()
    wv = nc.dram_tensor("wv", [NCH, P, D], F32, kind="ExternalInput").ap()
    wd = nc.dram_tensor("wd", [NCH, P, D], F32, kind="ExternalInput").ap()
    wgate = nc.dram_tensor("wgate", [NFC, P, D], F32, kind="ExternalInput").ap()
    wup = nc.dram_tensor("wup", [NFC, P, D], F32, kind="ExternalInput").ap()
    wout = nc.dram_tensor("wout", [2, NFC, P, 512], F32,
                           kind="ExternalInput").ap()
    wr = nc.dram_tensor("wr", [NCH, P, D], F32, kind="ExternalInput").ap()
    wk = nc.dram_tensor("wk", [NCH, P, D], F32, kind="ExternalInput").ap()
    wvf = nc.dram_tensor("wvf", [NCH, P, D], F32, kind="ExternalInput").ap()
    bmat = nc.dram_tensor("bmat", [NCH, P, D], F32, kind="ExternalInput").ap()
    cmat = nc.dram_tensor("cmat", [NCH, P, D], F32, kind="ExternalInput").ap()
    amat = nc.dram_tensor("amat", [NCH, P, D], F32, kind="ExternalInput").ap()
    vecs = nc.dram_tensor("vecs", [P, 80], F32, kind="ExternalInput").ap()

    import contextlib
    with contextlib.ExitStack() as S:
        acols = max(NCH * T, 8 * (TT + 1) + 8 + 5 * 8 * TT, 32 * TT)
        xsb = S.enter_context(nc.sbuf_tensor("xsb", [P, NCH * T], F32))
        arena = S.enter_context(nc.sbuf_tensor("arena", [P, acols], F32))
        prod = S.enter_context(nc.sbuf_tensor("prod", [P, NFC * TT], F32))
        ahi = S.enter_context(nc.sbuf_tensor("ahi", [P, NCH * D], BF16))
        alo = S.enter_context(nc.sbuf_tensor("alo", [P, NCH * D], BF16))
        wbuf0 = S.enter_context(nc.sbuf_tensor("wbuf0", [P, D], F32))
        wbuf1 = S.enter_context(nc.sbuf_tensor("wbuf1", [P, D], F32))
        wbuf2 = S.enter_context(nc.sbuf_tensor("wbuf2", [P, D], F32))
        sqb0 = S.enter_context(nc.sbuf_tensor("sqb0", [P, TT], F32))
        sqb1 = S.enter_context(nc.sbuf_tensor("sqb1", [P, TT], F32))
        vsb = S.enter_context(nc.sbuf_tensor("vsb", [P, 80], F32))
        ones = S.enter_context(nc.sbuf_tensor("ones", [P, 1], F32))
        onesk = S.enter_context(nc.sbuf_tensor("onesk", [1, P], F32))
        srow = S.enter_context(nc.sbuf_tensor("srow", [1, 4 * TT], F32))
        carry = S.enter_context(nc.sbuf_tensor("carry", [P, NCH], F32))
        hhl = S.enter_context(nc.sbuf_tensor("hhl", [P, 4 * NCH], BF16))
        hf = S.enter_context(nc.sbuf_tensor("hf", [P, NCH], F32))
        ztmp = S.enter_context(nc.sbuf_tensor("ztmp", [P, NCH], F32))
        ztmp2 = S.enter_context(nc.sbuf_tensor("ztmp2", [P, NCH], F32))
        evb = S.enter_context(nc.sbuf_tensor("evb", [P, 2 * TT], F32))
        junk = S.enter_context(nc.sbuf_tensor("junk", [P, 8], F32))
        ps = S.enter_context(nc.psum_tensor("ps", [P, 4096], F32))

        sems = {}
        for name in ("ld", "wd0", "wd1", "wd2", "wd3", "pe", "ve", "ac", "smm",
                     "szv", "sth", "sep", "sbf", "hcp", "st"):
            sems[name] = S.enter_context(nc.semaphore(name))

        plan = Plan()
        for n in sems:
            plan.sem(n)

        emit_program(plan, dict(
            xin=xin, xout=xout, wg=wg, wv=wv, wd=wd, wgate=wgate, wup=wup,
            wout=wout, wr=wr, wk=wk, wvf=wvf, bmat=bmat, cmat=cmat, amat=amat,
            vecs=vecs, xsb=xsb, arena=arena, prod=prod, ahi=ahi, alo=alo,
            wbuf=[wbuf0, wbuf1, wbuf2], sqb=[sqb0, sqb1], vsb=vsb,
            ones=ones,
            onesk=onesk, srow=srow, carry=carry, hhl=hhl, hf=hf, ztmp=ztmp,
            ztmp2=ztmp2, evb=evb, junk=junk, ps=ps))

        block = S.enter_context(nc.Block())

        def mk(eng_name):
            drainy = eng_name in DRAIN_ENGINES
            def body(e):
                first = True
                for (g, emit, waits, incs) in plan.ops:
                    if g != eng_name:
                        continue
                    for (s, v) in waits:
                        e.wait_ge(sems[s], v)
                    if drainy and not first:
                        e.drain()
                    first = False
                    inst = emit(e)
                    for (s, a) in incs:
                        inst.then_inc(sems[s], a)
            return body

        block.tensor(mk("tensor"))
        block.vector(mk("vector"))
        block.scalar(mk("scalar"))
        block.sync(mk("sync"))
        block.gpsimd(mk("gpsimd"))
    return nc


def emit_program(plan, t):
    xsb, arena, prod = t["xsb"], t["arena"], t["prod"]
    ahi, alo = t["ahi"], t["alo"]
    wbuf, sqb = t["wbuf"], t["sqb"]
    vsb, ones, onesk, srow = t["vsb"], t["ones"], t["onesk"], t["srow"]
    carry, hhl, hf = t["carry"], t["hhl"], t["hf"]
    ztmp, ztmp2, evb = t["ztmp"], t["ztmp2"], t["evb"]
    junk = t["junk"]

    def vdummy():
        plan.add("vector", lambda e: e.memset(junk[:, 0:8], 0.0))
    ps = t["ps"]

    NT_ = T // TT

    def bank(j, lo, n):
        return ps[:, j * 512 + lo: j * 512 + lo + n]

    def bankrow(j, lo, n):
        return ps[0:1, j * 512 + lo: j * 512 + lo + n]

    def bank_scan(j, n):
        return ps[:, j * 512: j * 512 + n]

    def xch(c, t0, n):
        return xsb[:, c * T + t0: c * T + t0 + n]

    def ach(c, t0, n):
        return arena[:, c * T + t0: c * T + t0 + n]

    state = {
        "wbuf_free": [[("ld", 0)], [("ld", 0)], [("ld", 0)]],
        "wrot": 0,
        "last_sq_pe": 0,
        "stats_free": 0,
    }

    for c in range(NCH):
        plan.add("sync", (lambda e, c=c: e.dma_start(
            out=xch(c, 0, T), in_=t["xin"][c, :, :])), incs=[("ld", 16)])
    plan.add("sync", lambda e: e.dma_start(out=vsb[:, :], in_=t["vecs"][:, :]),
             incs=[("ld", 16)])
    ld_done = plan.count["ld"]
    plan.add("vector", lambda e: e.memset(ones[:, :], 1.0))
    plan.add("vector", lambda e: e.memset(onesk[:, :], 1.0))
    plan.add("vector", lambda e: e.memset(carry[:, :], 0.0))
    plan.add("vector", lambda e: e.memset(t["hhl"][:, :], 0.0),
             waits=[("ld", ld_done)], incs=[("ve", 1)])
    init_tok = plan.count["ve"]


    def stats_pass(src_ap_fn, tile0, n, eps, want_mean, wait_pairs):
        srcs = [src_ap_fn(c) for c in range(NCH)]
        sq_consumed = [state["last_sq_pe"], state["last_sq_pe"]]
        for c in range(NCH):
            sq = sqb[c % 2]
            w = list(wait_pairs) if c == 0 else []
            if sq_consumed[c % 2]:
                w.append(("pe", sq_consumed[c % 2]))
            plan.add("scalar", (lambda e, src=srcs[c], sq=sq: e.square(
                out=sq[:, 0:n], in_=src)), waits=w, incs=[("ac", 1)])
            ac_tok = plan.count["ac"]
            w2 = [("ac", ac_tok)]
            wfree = ([("ve", state["stats_free"])]
                     if (c == 0 and state["stats_free"]) else [])
            plan.add("tensor", (lambda e, c=c, src=srcs[c]: e.matmul(
                bankrow(0, 0, n), ones[:, :], src,
                start=(c == 0), stop=(c == NCH - 1))),
                waits=(list(wait_pairs) + wfree) if c == 0 else [])
            tok = plan.add("tensor", (lambda e, c=c, sq=sq: e.matmul(
                bankrow(1, 0, n), ones[:, :], sq[:, 0:n],
                start=(c == 0), stop=(c == NCH - 1))), waits=w2,
                incs=[("pe", 1)])
            sq_consumed[c % 2] = tok
        state["last_sq_pe"] = plan.count["pe"]
        pe_tok = plan.count["pe"]
        mu = srow[0:1, 0:n]
        var = srow[0:1, n:2 * n]
        rstd = srow[0:1, 2 * n:3 * n]
        tmp = srow[0:1, 3 * n:4 * n]
        plan.add("vector", lambda e: e.tensor_scalar(
            out=mu, in0=bankrow(0, 0, n), scalar1=1.0 / D, scalar2=None,
            op0=OP.mult), waits=[("pe", pe_tok)])
        plan.add("vector", lambda e: e.tensor_scalar(
            out=var, in0=bankrow(1, 0, n), scalar1=1.0 / D, scalar2=eps,
            op0=OP.mult, op1=OP.add))
        if want_mean:
            plan.add("vector", lambda e: e.tensor_tensor(
                out=tmp, in0=mu, in1=mu, op=OP.mult))
            plan.add("vector", lambda e: e.drain())
            plan.add("vector", lambda e: e.tensor_tensor(
                out=var, in0=var, in1=tmp, op=OP.subtract))
        plan.add("vector", lambda e: e.drain())
        plan.add("vector", lambda e: e.reciprocal(out=rstd, in_=var),
                 incs=[("ve", 1)])
        ve_tok = plan.count["ve"]
        plan.add("scalar", lambda e: e.sqrt(out=rstd, in_=rstd),
                 waits=[("ve", ve_tok)], incs=[("ac", 1)])
        ac_tok = plan.count["ac"]
        plan.add("vector", lambda e: e.tensor_tensor(
            out=tmp, in0=rstd, in1=rstd, op=OP.mult), waits=[("ac", ac_tok)])
        plan.add("vector", lambda e: e.drain())
        plan.add("vector", lambda e: e.tensor_tensor(
            out=tmp, in0=tmp, in1=var, op=OP.mult))
        plan.add("vector", lambda e: e.drain())
        plan.add("vector", lambda e: e.tensor_scalar(
            out=tmp, in0=tmp, scalar1=-0.5, scalar2=1.5, op0=OP.mult,
            op1=OP.add))
        plan.add("vector", lambda e: e.drain())
        plan.add("vector", lambda e: e.tensor_tensor(
            out=rstd, in0=rstd, in1=tmp, op=OP.mult), incs=[("ve", 1)])
        ve2 = plan.count["ve"]
        if want_mean:
            plan.add("vector", lambda e: e.drain())
            plan.add("vector", lambda e: e.tensor_tensor(
                out=tmp, in0=mu, in1=rstd, op=OP.mult))
            plan.add("vector", lambda e: e.drain())
            plan.add("vector", lambda e: e.tensor_scalar(
                out=tmp, in0=tmp, scalar1=-1.0, scalar2=None, op0=OP.mult),
                incs=[("ve", 1)])
            ve2 = plan.count["ve"]
        plan.add("tensor", lambda e: e.matmul(
            bank(1, 0, n), onesk[:, :], rstd, start=True,
            stop=not want_mean), waits=[("ve", ve2)],
            incs=([("pe", 1)] if not want_mean else []))
        if want_mean:
            plan.add("tensor", lambda e: e.matmul(
                bank(1, n, n), onesk[:, :], tmp, start=False, stop=True),
                incs=[("pe", 1)])
        return ("pe", plan.count["pe"])

    def stream_w(dram, chunk):
        which = state["wrot"]
        state["wrot"] = (which + 1) % 3
        sem = "wd%d" % which
        eng = "sync" if which % 2 == 0 else "gpsimd"
        tok = plan.add(eng, (lambda e, dram=dram, chunk=chunk, which=which:
                             e.dma_start(out=wbuf[which][:, :],
                                         in_=dram[chunk, :, :])),
                       waits=state["wbuf_free"][which] +
                       [(sem, plan.count[sem])],
                       incs=[(sem, 16)])
        return [(sem, tok)], which

    def mm_group(out_ap, stat_sb, moving_fn, wait_pairs, extra_last=()):
        movings = [moving_fn(c) for c in range(NCH)]
        for c in range(NCH):
            w = list(wait_pairs) if c == 0 else []
            plan.add("tensor", (lambda e, c=c, m=movings[c]: e.matmul(
                out_ap, stat_sb[:, c * P:(c + 1) * P], m,
                start=(c == 0), stop=(c == NCH - 1))),
                waits=w, incs=([("pe", 1)] if c == NCH - 1 else []))
        return ("pe", plan.count["pe"])

    ph1_wait = [("ld", ld_done), ("ve", init_tok)]

    evb_free = [("ld", 0), ("ld", 0)]
    ln_done_tok = {}

    for ti in range(NT_):
        t0 = ti * TT
        w0 = ph1_wait if ti == 0 else []

        btok = stats_pass(lambda c: xch(c, t0, TT), t0, TT, EPS_LN, True, w0)

        for c in range(NCH):
            plan.add("vector", (lambda e, o=ach(c, t0, TT),
                                i0=xch(c, t0, TT): e.tensor_tensor(
                out=o, in0=i0, in1=bank(1, 0, TT),
                op=OP.mult)), waits=[btok] if c == 0 else [])
        for c in range(NCH):
            plan.add("vector", (lambda e, o=ach(c, t0, TT): e.tensor_tensor(
                out=o, in0=o, in1=bank(1, TT, TT),
                op=OP.add)))
        plan.add("vector", lambda e: e.drain(), incs=[("ve", 1)])
        ln_tok = ("ve", plan.count["ve"])
        state["stats_free"] = plan.count["ve"]

        btok = stats_pass(lambda c: ach(c, t0, TT), t0, TT, EPS_RMS, False,
                          [ln_tok])
        for c in range(NCH):
            plan.add("vector", (lambda e, c=c, i0=ach(c, t0, TT):
                                e.tensor_tensor(
                out=prod[:, c * TT:(c + 1) * TT], in0=i0,
                in1=bank(1, 0, TT), op=OP.mult)),
                waits=[btok] if c == 0 else [])
        for c in range(NCH):
            plan.add("vector", (lambda e, c=c: e.tensor_scalar(
                out=prod[:, c * TT:(c + 1) * TT],
                in0=prod[:, c * TT:(c + 1) * TT],
                scalar1=vsb[:, 24 + c:25 + c], scalar2=None, op0=OP.mult)))
        plan.add("vector", lambda e: e.drain(), incs=[("ve", 1)])
        xn_tok = ("ve", plan.count["ve"])
        state["stats_free"] = plan.count["ve"]

        if ti == 0:
            gvd_prev = [0, 0, 0]
            xs_prev = 0
        for j in range(NCH):
            for m, dram in enumerate((t["wg"], t["wv"], t["wd"])):
                wtok, which = stream_w(dram, j)
                bk = 2 + m
                mmtok = mm_group(
                    bank(bk, 0, TT), wbuf[which],
                    (lambda c, j=j: prod[:, c * TT:(c + 1) * TT]),
                    wtok + [xn_tok] + ([("ac", gvd_prev[m])]
                                      if gvd_prev[m] else []))
                state["wbuf_free"][which] = [mmtok]
                ev_guard = [("ve", xs_prev)] if (m < 2 and xs_prev) else []
                if m == 0:
                    plan.add("scalar", (lambda e, j=j: e.activation(
                        out=evb[:, 0:TT], in_=bank(2, 0, TT), func=AF.Sigmoid,
                        bias=vsb[:, 0 + j:1 + j], scale=1.0)),
                        waits=[mmtok] + ev_guard, incs=[("ac", 1)])
                elif m == 1:
                    plan.add("scalar", (lambda e, j=j: e.activation(
                        out=evb[:, TT:2 * TT], in_=bank(3, 0, TT),
                        func=AF.Tanh, bias=vsb[:, 8 + j:9 + j], scale=1.0)),
                        waits=[mmtok] + ev_guard, incs=[("ac", 1)])
                else:
                    plan.add("scalar", (lambda e, j=j: e.activation(
                        out=prod[:, (16 + j) * TT:(17 + j) * TT],
                        in_=bank(4, 0, TT), func=AF.Sigmoid,
                        bias=vsb[:, 16 + j:17 + j], scale=1.0)),
                        waits=[mmtok], incs=[("ac", 1)])
                gvd_prev[m] = plan.count["ac"]
            ac_j = plan.count["ac"]
            xs_prev = plan.add("vector", (lambda e, j=j: e.tensor_tensor(
                out=prod[:, (8 + j) * TT:(9 + j) * TT], in0=evb[:, 0:TT],
                in1=evb[:, TT:2 * TT], op=OP.mult)),
                waits=[("ac", ac_j)], incs=[("ve", 1)])
            plan.add("vector", (lambda e, j=j: e.tensor_scalar(
                out=prod[:, (16 + j) * TT:(17 + j) * TT],
                in0=prod[:, (16 + j) * TT:(17 + j) * TT],
                scalar1=0.998, scalar2=0.001, op0=OP.mult, op1=OP.add)),
                incs=[("ve", 1)])
        for c in range(NCH):
            plan.add("vector", (lambda e, c=c: e.tensor_tensor_scan(
                out=evb[:, 0:TT], data0=prod[:, (16 + c) * TT:(17 + c) * TT],
                data1=prod[:, (8 + c) * TT:(9 + c) * TT],
                initial=carry[:, c:c + 1], op0=OP.mult, op1=OP.add)))
            plan.add("vector", lambda e: e.drain())
            plan.add("vector", (lambda e, c=c: e.tensor_copy(
                out=carry[:, c:c + 1], in_=evb[:, TT - 1:TT])))
            plan.add("vector", (lambda e, o=ach(c, t0, TT): e.tensor_tensor(
                out=o, in0=o, in1=evb[:, 0:TT],
                op=OP.add)))
            plan.add("vector", lambda e: e.drain())
            plan.add("vector", (lambda e, o=xch(c, t0, TT),
                                i1=ach(c, t0, TT): e.tensor_tensor(
                out=o, in0=o, in1=i1,
                op=OP.add)))
        plan.add("vector", lambda e: e.drain(), incs=[("ve", 1)])
        ph1_wait = [("ve", plan.count["ve"])]

    mix_done = [("ve", plan.count["ve"])]
    if NPHASE == 1:
        return finish(plan, t, xch, plan.count["ve"])

    for ti in range(NT_):
        t0 = ti * TT
        w0 = mix_done if ti == 0 else []
        btok = stats_pass(lambda c: ach(c, t0, TT), t0, TT, EPS_RMS, False, w0)
        for c in range(NCH):
            plan.add("vector", (lambda e, o=ach(c, t0, TT): e.tensor_tensor(
                out=o, in0=o, in1=bank(1, 0, TT),
                op=OP.mult)), waits=[btok] if c == 0 else [])
        for c in range(NCH):
            plan.add("vector", (lambda e, c=c, o=ach(c, t0, TT):
                                e.tensor_scalar(
                out=o, in0=o,
                scalar1=vsb[:, 32 + c:33 + c], scalar2=None, op0=OP.mult)))
        plan.add("vector", lambda e: e.drain(), incs=[("ve", 1)])
        xn2_tok = ("ve", plan.count["ve"])
        state["stats_free"] = plan.count["ve"]

        if ti == 0:
            silu_prev = None
            prod_prev = None
            evac_prev = None
        for fj in range(NFC):
            wt_g, bg_ = stream_w(t["wgate"], fj)
            tok_g = mm_group(bank(2, 0, TT), wbuf[bg_],
                             (lambda c: ach(c, t0, TT)),
                             wt_g + [xn2_tok] +
                             ([silu_prev] if silu_prev else []))
            state["wbuf_free"][bg_] = [tok_g]
            plan.add("scalar", (lambda e: e.activation(
                out=evb[:, 0:TT], in_=bank(2, 0, TT), func=AF.Silu)),
                waits=[tok_g] + ([prod_prev] if prod_prev else []),
                incs=[("ac", 1)])
            silu_prev = ("ac", plan.count["ac"])
            wt_u, bu_ = stream_w(t["wup"], fj)
            tok_u = mm_group(bank(3, 0, TT), wbuf[bu_],
                             (lambda c: ach(c, t0, TT)),
                             wt_u + ([prod_prev] if prod_prev else []))
            state["wbuf_free"][bu_] = [tok_u]
            plan.add("vector", (lambda e, fj=fj: e.tensor_tensor(
                out=prod[:, fj * TT:(fj + 1) * TT], in0=evb[:, 0:TT],
                in1=bank(3, 0, TT), op=OP.mult)),
                waits=[tok_u, silu_prev], incs=[("ve", 1)])
            prod_prev = ("ve", plan.count["ve"])

        for half in range(2):
            for fj in range(NFC):
                which = fj % 2
                sem_o = "wd%d" % which
                wt_o = plan.add("sync" if which == 0 else "gpsimd",
                                (lambda e, fj=fj, half=half,
                                         which=which: e.dma_start(
                    out=wbuf[which][:, 0:512],
                    in_=t["wout"][half, fj, :, :])),
                    waits=state["wbuf_free"][which] +
                    [(sem_o, plan.count[sem_o])],
                    incs=[(sem_o, 16)])
                for dc in range(4):
                    w = []
                    if dc == 0:
                        w.append(("wd%d" % which, wt_o))
                    if fj == 0 and dc == 0:
                        w.append(prod_prev)
                        if evac_prev is not None:
                            w.append(evac_prev)
                    tok = plan.add("tensor", (lambda e, fj=fj, dc=dc,
                                              which=which: e.matmul(
                        bank(4 + dc, 0, TT),
                        wbuf[which][:, dc * P:(dc + 1) * P],
                        prod[:, fj * TT:(fj + 1) * TT],
                        start=(fj == 0), stop=(fj == NFC - 1))),
                        waits=w,
                        incs=([("pe", 1)] if dc == 3 else []))
                state["wbuf_free"][which] = [("pe", plan.count["pe"])]
            accum_tok = ("pe", plan.count["pe"])
            for dc in range(4):
                c = half * 4 + dc
                plan.add("vector", (lambda e, dc=dc, o=xch(c, t0, TT):
                                    e.tensor_tensor(
                    out=o, in0=o,
                    in1=bank(4 + dc, 0, TT), op=OP.add)),
                    waits=[accum_tok] if dc == 0 else [])
            plan.add("vector", lambda e: e.drain(), incs=[("ve", 1)])
            evac_prev = ("ve", plan.count["ve"])

    ffn1_done = [("ve", plan.count["ve"])]
    if NPHASE == 2:
        return finish(plan, t, xch, plan.count["ve"])

    def emit_ffn(begin_wait):
        LNB = 0
        XR = 2056 + 8
        XK = XR + 8 * TT
        TMP = XK + 8 * TT
        RB = TMP + 8 * TT
        KA = RB + 8 * TT
        last = KA + 8 * TT

        def lnb(c, o, n):
            return arena[:, LNB + c * (TT + 1) + o: LNB + c * (TT + 1) + o + n]

        def ar(base, c, n=TT):
            return arena[:, base + c * TT: base + c * TT + n]

        wv = begin_wait
        for ti in range(NT_):
            t0 = ti * TT
            w0 = wv if ti == 0 else []
            for c in range(NCH):
                if ti == 0:
                    plan.add("vector", (lambda e, c=c: e.memset(
                        lnb(c, 0, 1), 0.0)), waits=w0 if c == 0 else [])
                else:
                    plan.add("vector", (lambda e, c=c: e.tensor_copy(
                        out=lnb(c, 0, 1), in_=lnb(c, TT, 1))))
            plan.add("vector", lambda e: e.drain(), incs=[("ve", 1)])
            sv_tok = ("ve", plan.count["ve"])
            btok = stats_pass(lambda c: xch(c, t0, TT), t0, TT, EPS_LN, True,
                              [sv_tok])
            for c in range(NCH):
                plan.add("vector", (lambda e, c=c, i0=xch(c, t0, TT):
                                    e.tensor_tensor(
                    out=lnb(c, 1, TT), in0=i0,
                    in1=bank(1, 0, TT), op=OP.mult)),
                    waits=[btok] if c == 0 else [])
            for c in range(NCH):
                plan.add("vector", (lambda e, c=c: e.tensor_tensor(
                    out=lnb(c, 1, TT), in0=lnb(c, 1, TT), in1=bank(1, TT, TT),
                    op=OP.add)))
            for c in range(NCH):
                plan.add("vector", (lambda e, c=c: e.tensor_tensor(
                    out=ar(TMP, c), in0=lnb(c, 1, TT), in1=lnb(c, 0, TT),
                    op=OP.subtract)))
            for c in range(NCH):
                plan.add("vector", (lambda e, c=c: e.scalar_tensor_tensor(
                    out=ar(XR, c), in0=ar(TMP, c), scalar=vsb[:, 40 + c:41 + c],
                    in1=lnb(c, 0, TT), op0=OP.mult, op1=OP.add)))
            for c in range(NCH):
                plan.add("vector", (lambda e, c=c: e.scalar_tensor_tensor(
                    out=ar(XK, c), in0=ar(TMP, c), scalar=vsb[:, 48 + c:49 + c],
                    in1=lnb(c, 0, TT), op0=OP.mult, op1=OP.add)))
            plan.add("vector", lambda e: e.drain(), incs=[("ve", 1)])
            xrk_tok = ("ve", plan.count["ve"])
            state["stats_free"] = plan.count["ve"]

            for j in range(NCH):
                wt_r, br_ = stream_w(t["wr"], j)
                tok_r = mm_group(bank(2, 0, TT), wbuf[br_],
                                 (lambda c: ar(XR, c)), wt_r + [xrk_tok] +
                                 ([] if (ti == 0 and j == 0) else [r_prev]))
                state["wbuf_free"][br_] = [tok_r]
                plan.add("scalar", (lambda e, j=j: e.activation(
                    out=ar(RB, j), in_=bank(2, 0, TT), func=AF.Sigmoid)),
                    waits=[tok_r], incs=[("ac", 1)])
                r_prev = ("ac", plan.count["ac"])
                wt_k, bk_ = stream_w(t["wk"], j)
                tok_k = mm_group(bank(3, 0, TT), wbuf[bk_],
                                 (lambda c: ar(XK, c)), wt_k +
                                 ([] if (ti == 0 and j == 0) else [k_prev]))
                state["wbuf_free"][bk_] = [tok_k]
                plan.add("scalar", (lambda e, j=j: e.activation(
                    out=evb[:, 0:TT], in_=bank(3, 0, TT), func=AF.Relu)),
                    waits=[tok_k], incs=[("ac", 1)])
                plan.add("scalar", (lambda e, j=j: e.square(
                    out=ar(KA, j), in_=evb[:, 0:TT])), incs=[("ac", 1)])
                k_prev = ("ac", plan.count["ac"])
            ka_tok = k_prev

            for j in range(NCH):
                wt_v, bv_ = stream_w(t["wvf"], j)
                tok_v = mm_group(bank(2 + (j % 2), 0, TT), wbuf[bv_],
                                 (lambda c: ar(KA, c)), wt_v + [ka_tok] +
                                 ([] if (ti == 0 and j == 0) else [o_prev]))
                state["wbuf_free"][bv_] = [tok_v]
                plan.add("vector", (lambda e, j=j: e.tensor_tensor(
                    out=ar(TMP, j), in0=ar(RB, j),
                    in1=bank(2 + (j % 2), 0, TT), op=OP.mult)),
                    waits=[tok_v], incs=[("ve", 1)])
                plan.add("vector", lambda e: e.drain())
                plan.add("vector", (lambda e, j=j, o=xch(j, t0, TT):
                                    e.tensor_tensor(
                    out=o, in0=o,
                    in1=ar(TMP, j), op=OP.add)), incs=[("ve", 1)])
                o_prev = ("ve", plan.count["ve"])
            wv = [o_prev]
        return [("ve", plan.count["ve"])]

    ffn2_done = emit_ffn(ffn1_done)
    if NPHASE == 3:
        return finish(plan, t, xch, plan.count["ve"])

    plan.add("sync", lambda e: e.nop(),
             waits=[("wd%d" % k, plan.count["wd%d" % k]) for k in range(4)])
    for g in range(2):
        for c in range(4):
            ch = g * 4 + c
            tok = plan.add("sync", (lambda e, ch=ch, c=c: e.dma_start(
                out=prod[:, c * D:(c + 1) * D], in_=t["amat"][ch, :, :])),
                waits=(list(ffn2_done) if (g == 0 and c == 0) else
                       [sp_tok] if c == 0 else []) +
                [("wd0", plan.count["wd0"])], incs=[("wd0", 16)])
        wd_tok = plan.count["wd0"]
        for c in range(4):
            ch = g * 4 + c
            plan.add("vector", (lambda e, ch=ch, c=c: e.tensor_copy(
                out=ahi[:, ch * D:(ch + 1) * D],
                in_=prod[:, c * D:(c + 1) * D])),
                waits=[("wd0", wd_tok)] if c == 0 else [])
        for c in range(4):
            ch = g * 4 + c
            plan.add("vector", (lambda e, ch=ch, c=c: e.tensor_tensor(
                out=alo[:, ch * D:(ch + 1) * D],
                in0=prod[:, c * D:(c + 1) * D],
                in1=ahi[:, ch * D:(ch + 1) * D], op=OP.subtract)))
        plan.add("vector", lambda e: e.drain(), incs=[("ve", 1)])
        sp_tok = ("ve", plan.count["ve"])
    a_ready = sp_tok

    for j in range(6):
        plan.add("sync", (lambda e, j=j: e.dma_start(
            out=prod[:, j * D:(j + 1) * D], in_=t["cmat"][j, :, :])),
            waits=([a_ready, ("wd0", plan.count["wd0"])] if j == 0 else []),
            incs=[("wd0", 16)])
    for j in range(6, NCH):
        plan.add("sync", (lambda e, j=j: e.dma_start(
            out=arena[:, 10240 + (j - 6) * D:10240 + (j - 5) * D],
            in_=t["cmat"][j, :, :])), incs=[("wd0", 16)])
    bc_tok = ("wd0", plan.count["wd0"])

    def cstat(j):
        if j < 6:
            return prod[:, j * D:(j + 1) * D]
        return arena[:, 10240 + (j - 6) * D:10240 + (j - 5) * D]

    XLN = 0
    XB = 8 * TT
    HB = 16 * TT
    PRE = 24 * TT

    def arn(base, c, n=TT):
        return arena[:, base + c * TT: base + c * TT + n]

    gs = 0
    ph3_wait = [a_ready]
    ys_prev = None
    for ti in range(NT_):
        t0 = ti * TT
        w0 = ph3_wait if ti == 0 else []
        btok = stats_pass(lambda c: xch(c, t0, TT), t0, TT, EPS_LN, True, w0)
        for c in range(NCH):
            plan.add("vector", (lambda e, c=c, i0=xch(c, t0, TT):
                                e.tensor_tensor(
                out=arn(XLN, c), in0=i0, in1=bank(1, 0, TT),
                op=OP.mult)), waits=[btok] if c == 0 else [])
        for c in range(NCH):
            plan.add("vector", (lambda e, c=c: e.tensor_tensor(
                out=arn(XLN, c), in0=arn(XLN, c), in1=bank(1, TT, TT),
                op=OP.add)))
        plan.add("vector", lambda e: e.drain(), incs=[("ve", 1)])
        xln_tok = ("ve", plan.count["ve"])
        state["stats_free"] = plan.count["ve"]

        for j in range(NCH):
            wt_b, bb_ = stream_w(t["bmat"], j)
            tok_b = mm_group(bank(2 + (j % 2), 0, TT), wbuf[bb_],
                             (lambda c: arn(XLN, c)),
                             wt_b + [xln_tok] +
                             ([bc_tok] if (ti == 0 and j == 0) else []) +
                             ([] if (ti == 0 and j <= 1) else [xb_prev]))
            state["wbuf_free"][bb_] = [tok_b]
            plan.add("vector", (lambda e, j=j: e.tensor_copy(
                out=arn(XB, j), in_=bank(2 + (j % 2), 0, TT))),
                waits=[tok_b], incs=[("ve", 1)])
            xb_prev = ("ve", plan.count["ve"])
        xb_tok = xb_prev

        for s in range(TT):
            br = (gs % 2) * 2 * NCH
            bw = ((gs + 1) % 2) * 2 * NCH

            def scan_mm(j, i, w, start, stop, smm_inc):
                til = j * NCH + i
                plan.add("tensor", (lambda e, til=til, br=br, i=i,
                                    start=start: e.matmul(
                    bank_scan(til // NCH, 2),
                    ahi[:, til * P:(til + 1) * P],
                    hhl[:, br + i * 2:br + i * 2 + 2],
                    start=start, stop=False)),
                    waits=w)
                plan.add("tensor", (lambda e, til=til, br=br, i=i,
                                    stop=stop: e.matmul(
                    bank_scan(til // NCH, 1),
                    alo[:, til * P:(til + 1) * P],
                    hhl[:, br + i * 2:br + i * 2 + 1], start=False,
                    stop=stop)),
                    incs=([("smm", 1)] if smm_inc else []))

            for j in range(4, NCH):
                for i in range(4, NCH):
                    if j == 4 and i == 4 and gs > 0:
                        w0_ = [("sth", 2 * gs - 1)]
                        if s == 0:
                            w0_.append(xb_tok)
                        til = 4 * NCH + 4
                        plan.add("tensor", (lambda e, til=til, br=br:
                                 e.matmul(
                            bank_scan(4, 1),
                            ahi[:, til * P:(til + 1) * P],
                            hhl[:, br + 8:br + 9], start=True, stop=False)),
                            waits=w0_)
                        plan.add("tensor", (lambda e, til=til, br=br:
                                 e.matmul(
                            bank_scan(4, 1),
                            alo[:, til * P:(til + 1) * P],
                            hhl[:, br + 8:br + 9], start=False, stop=False)))
                        plan.add("tensor", (lambda e, til=til, br=br:
                                 e.matmul(
                            ps[:, 4 * 512 + 1:4 * 512 + 2],
                            ahi[:, til * P:(til + 1) * P],
                            hhl[:, br + 9:br + 10], start=False, stop=False)),
                            waits=[("sep", 2 * gs - 1)])
                        continue
                    w = []
                    if j == 4 and i == 4:
                        w = [xb_tok, a_ready]
                    scan_mm(j, i, w, start=(i == 4), stop=False, smm_inc=False)
            for j in range(4, NCH):
                for i in range(4):
                    w = [("sep", 2 * gs)] if (j == 4 and i == 0 and gs > 0) \
                        else []
                    scan_mm(j, i, w, start=False, stop=(i == 3),
                            smm_inc=(j == NCH - 1 and i == 3))
            for j in range(4):
                for i in range(4, NCH):
                    w = [("sbf", gs)] if (j == 0 and i == 4 and gs > 0) else []
                    scan_mm(j, i, w, start=(i == 4), stop=False, smm_inc=False)
            for j in range(4):
                for i in range(4):
                    scan_mm(j, i, [], start=False, stop=(i == 3),
                            smm_inc=(j == 3 and i == 3))

            hap = [arena[:, HB + 4 * h * TT + s: HB + (4 * h + 3) * TT
                          + s + 1:TT] for h in range(2)]
            for h in (1, 0):
                plan.add("vector", (lambda e, h=h, s=s: e.tensor_tensor(
                    out=ztmp[:, 4 * h:4 * h + 4],
                    in_=None, in0=ps[:, 2048 * h + 1:2048 * h + 1538:512],
                    in1=arena[:, XB + 4 * h * TT + s: XB + (4 * h + 3) * TT
                              + s + 1:TT],
                    op=OP.add) if False else e.tensor_tensor(
                    out=ztmp[:, 4 * h:4 * h + 4],
                    in0=ps[:, 2048 * h + 1:2048 * h + 1538:512],
                    in1=arena[:, XB + 4 * h * TT + s: XB + (4 * h + 3) * TT
                              + s + 1:TT],
                    op=OP.add)),
                    waits=[("smm", 2 * gs + (1 if h == 1 else 2))])
                vdummy()
                plan.add("vector", (lambda e, h=h: e.tensor_tensor(
                    out=ztmp2[:, 4 * h:4 * h + 4],
                    in0=ps[:, 2048 * h:2048 * h + 1537:512],
                    in1=ztmp[:, 4 * h:4 * h + 4],
                    op=OP.add)),
                    incs=[("szv", 1)])
                szv_v = plan.count["szv"]
                plan.add("scalar", (lambda e, h=h: e.activation(
                    out=ztmp2[:, 4 * h:4 * h + 4],
                    in_=ztmp2[:, 4 * h:4 * h + 4],
                    func=AF.Tanh)), waits=[("szv", szv_v)],
                    incs=([("sbf", 1)] if h == 0 else []))
                sth_v = plan.add("scalar", (lambda e, h=h, bw=bw: e.copy(
                    out=hhl[:, bw + 8 * h:bw + 8 * h + 7:2],
                    in_=ztmp2[:, 4 * h:4 * h + 4])),
                    incs=[("sth", 1)])
                plan.add("vector", (lambda e, h=h, bw=bw: e.tensor_tensor(
                    out=hhl[:, bw + 8 * h + 1:bw + 8 * h + 8:2],
                    in0=ztmp2[:, 4 * h:4 * h + 4],
                    in1=hhl[:, bw + 8 * h:bw + 8 * h + 7:2],
                    op=OP.subtract)),
                    waits=[("sth", sth_v)], incs=[("sep", 1)])
                plan.add("vector", (lambda e, h=h, ho=hap[h]: e.tensor_copy(
                    out=ho, in_=ztmp2[:, 4 * h:4 * h + 4])),
                    incs=[("hcp", 1)])
            gs += 1
        scan_tok = ("sep", plan.count["sep"])
        hcp_tok = ("hcp", plan.count["hcp"])

        for j in range(NCH):
            tok_c = mm_group(bank(2 + (j % 2), 0, TT), cstat(j),
                             (lambda c: arn(HB, c)), [scan_tok, hcp_tok] +
                             ([] if ys_prev is None else [ys_prev]))
            plan.add("vector", (lambda e, j=j: e.tensor_tensor(
                out=arn(PRE, j), in0=arn(XLN, j),
                in1=bank(2 + (j % 2), 0, TT), op=OP.add)),
                waits=[tok_c], incs=[("ve", 1)])
            ys_prev = ("ve", plan.count["ve"])

        btok = stats_pass(lambda c: arn(PRE, c), t0, TT, EPS_LN, True,
                          [ys_prev])
        T3T = 32 * TT
        for c in range(NCH):
            plan.add("vector", (lambda e, c=c: e.tensor_tensor(
                out=arn(T3T, c), in0=arn(PRE, c), in1=bank(1, 0, TT),
                op=OP.mult)), waits=[btok] if c == 0 else [])
        for c in range(NCH):
            plan.add("vector", (lambda e, c=c: e.tensor_tensor(
                out=arn(T3T, c), in0=arn(T3T, c), in1=bank(1, TT, TT),
                op=OP.add)))
        for c in range(NCH):
            plan.add("vector", (lambda e, c=c: e.tensor_scalar(
                out=arn(T3T, c), in0=arn(T3T, c),
                scalar1=vsb[:, 56 + c:57 + c], scalar2=vsb[:, 64 + c:65 + c],
                op0=OP.mult, op1=OP.add)))
        for c in range(NCH):
            plan.add("vector", (lambda e, c=c, o=xch(c, t0, TT):
                                e.tensor_tensor(
                out=o, in0=o, in1=arn(T3T, c),
                op=OP.add)))
        plan.add("vector", lambda e: e.drain(), incs=[("ve", 1)])
        ph3_wait = [("ve", plan.count["ve"])]
        state["stats_free"] = plan.count["ve"]

    attnb_done = [("ve", plan.count["ve"])]
    if NPHASE == 4:
        return finish(plan, t, xch, plan.count["ve"])

    emit_ffn(attnb_done)
    final_tok = plan.count["ve"]

    finish(plan, t, xch, final_tok)


def finish(plan, t, xch, final_tok):
    for c in range(NCH):
        plan.add("sync", (lambda e, c=c: e.dma_start(
            out=t["xout"][c, :, :], in_=xch(c, 0, T))),
            waits=[("ve", final_tok)] if c == 0 else [], incs=[("st", 16)])
    st_done = plan.count["st"]
    plan.add("sync", lambda e: e.nop(), waits=[("st", st_done)])




def _tileize(wt, n_out_chunks):
    K, M = wt.shape
    nk = K // P
    arr = wt.reshape(nk, P, n_out_chunks, P).transpose(2, 1, 0, 3)
    return np.ascontiguousarray(arr.reshape(n_out_chunks, P, K))


def _vecpack(d):
    v = np.zeros((P, 80), np.float32)
    names = ["bg", "bv", "bd", "rmw", "rfw", "mur", "muk", "gam", "bet"]
    for i, n in enumerate(names):
        v[:, i * 8:(i + 1) * 8] = d[n].reshape(NCH, P).T
    return v


def kernel(**inp):
    x = np.asarray(inp["L0_attn_a_x"], np.float32)
    B_, S_, D_ = x.shape
    assert (B_, S_, D_) == (8, T, D)

    wg = _tileize(np.ascontiguousarray(np.asarray(inp["L0_attn_a_Wg"]).T), NCH)
    wv = _tileize(np.ascontiguousarray(np.asarray(inp["L0_attn_a_Wv"]).T), NCH)
    wd = _tileize(np.ascontiguousarray(np.asarray(inp["L0_attn_a_Wd"]).T), NCH)
    wgate = _tileize(np.ascontiguousarray(np.asarray(inp["L0_attn_a_W_gate"]).T), NFC)
    wup = _tileize(np.ascontiguousarray(np.asarray(inp["L0_attn_a_W_up"]).T), NFC)
    wout_t = np.ascontiguousarray(np.asarray(inp["L0_attn_a_W_out"]).T)
    wout = np.ascontiguousarray(
        wout_t.reshape(NFC, P, 2, 512).transpose(2, 0, 1, 3))
    wr = _tileize(np.ascontiguousarray(np.asarray(inp["L0_ffn_W_r"]).T), NCH)
    wk = _tileize(np.ascontiguousarray(np.asarray(inp["L0_ffn_W_k"]).T), NCH)
    wvf = _tileize(np.ascontiguousarray(np.asarray(inp["L0_ffn_W_v"]).T), NCH)
    bmat = _tileize(np.ascontiguousarray(np.asarray(inp["L0_attn_b_B"]).T), NCH)
    cmat = _tileize(np.ascontiguousarray(np.asarray(inp["L0_attn_b_C"]).T), NCH)
    amat = _tileize(np.ascontiguousarray(np.asarray(inp["L0_attn_b_A"]).T), NCH)
    vecs = _vecpack(dict(
        bg=np.asarray(inp["L0_attn_a_bg"], np.float32),
        bv=np.asarray(inp["L0_attn_a_bv"], np.float32),
        bd=np.asarray(inp["L0_attn_a_bd"], np.float32),
        rmw=np.asarray(inp["L0_attn_a_rms_mix_w"], np.float32),
        rfw=np.asarray(inp["L0_attn_a_rms_ffn_w"], np.float32),
        mur=np.asarray(inp["L0_ffn_mu_r"], np.float32),
        muk=np.asarray(inp["L0_ffn_mu_k"], np.float32),
        gam=np.asarray(inp["L0_attn_b_gamma"], np.float32),
        bet=np.asarray(inp["L0_attn_b_beta"], np.float32),
    ))

    shared = dict(wg=wg, wv=wv, wd=wd, wgate=wgate, wup=wup, wout=wout,
                  wr=wr, wk=wk, wvf=wvf, bmat=bmat, cmat=cmat, amat=amat,
                  vecs=vecs)
    in_maps = []
    for b in range(8):
        xb = np.ascontiguousarray(x[b].T.reshape(NCH, P, T))
        m = dict(shared)
        m["xin"] = xb
        in_maps.append(m)

    nc = build_nc()
    res = run_bass_kernel_spmd(nc, in_maps, core_ids=list(range(8)))
    outs = []
    for b in range(8):
        o = res.results[b]["xout"].reshape(D, T).T
        outs.append(o)
    return np.stack(outs, axis=0).astype(np.float32)
